# version 02
# baseline (speedup 1.0000x reference)
import contextlib
import numpy as np

import concourse.bass as bass
import concourse.mybir as mybir
from concourse.bass_utils import run_bass_kernel_spmd

F32 = mybir.dt.float32
BF16 = mybir.dt.bfloat16
I16 = mybir.dt.int16
AF = mybir.ActivationFunctionType
ALU = mybir.AluOpType
AX = mybir.AxisListType

D = 1024
T = 2048
I = 3072
NST = 16
DC = D // 128
IC = I // 128
TT4 = 4
TQ = T // TT4
MAGIC = float(np.float32(3 * 2 ** 22))
EPS = 1e-5
NCORES = 8

(M_X, S_SX, S_RX, M_G, S_SG, S_RG, S_MSGM, M_U, S_SU, S_RU, M_P, S_SP, S_RPP,
 M_O, S_SO, S_ROP, S_K, S_T0, S_T1, S_TA, S_TB, S_TC, S_ONES_, S_PAD_) = range(24)
NSCAL = 24

DEBUG_STAGE = None
_DBG = {}


class Sem:
    def __init__(self, ctx, nc, name):
        self.h = ctx.enter_context(nc.semaphore(name))
        self.n = 0

    def add(self, k):
        self.n += k
        return self.n


class Prog:
    def __init__(self, debug_stage=None):
        self.debug_stage = debug_stage
        self.nc = bass.Bass("TRN2", target_bir_lowering=False, debug=False)
        self.ctx = contextlib.ExitStack()
        self.scopes = []
        self.ops = {e: [] for e in ("sync", "vector", "scalar", "tensor", "gpsimd")}
        self.phase_marks = []
        self.cc_idx = 0
        self._mk_io()
        self._mk_global_sbuf()
        self._mk_sems()

    def _dram(self, name, shape, dtype, kind="Internal"):
        return self.nc.dram_tensor(name, shape, dtype, kind=kind).ap()

    def sbuf(self, name, shape, dt=F32, scoped=False):
        if scoped:
            return self.scopes[-1].enter_context(self.nc.sbuf_tensor(name, shape, dt))
        return self.ctx.enter_context(self.nc.sbuf_tensor(name, shape, dt))

    def push_scope(self):
        self.scopes.append(contextlib.ExitStack())

    def pop_scope(self):
        self.scopes.pop().close()

    def _mk_io(self):
        d = self._dram
        self.x_in = d("x_fm", [128, DC, T], F32, "ExternalInput")
        self.wgT = d("wgT", [128, DC, I], F32, "ExternalInput")
        self.wuT = d("wuT", [128, DC, I], F32, "ExternalInput")
        self.wdT = d("wdT", [128, IC, D], F32, "ExternalInput")
        self.wgN = d("wgN", [128, IC, D], F32, "ExternalInput")
        self.wuN = d("wuN", [128, IC, D], F32, "ExternalInput")
        self.wdN = d("wdN", [128, DC, I], F32, "ExternalInput")
        self.wdelta = d("wdelta", [128, DC, D], F32, "ExternalInput")
        self.wB = d("wB", [128, DC, NST], F32, "ExternalInput")
        self.wC = d("wC", [128, DC, NST], F32, "ExternalInput")
        self.alog = d("alog", [128, DC * NST], F32, "ExternalInput")
        self.bdelta = d("bdelta", [128, DC], F32, "ExternalInput")
        self.bBC = d("bBC", [2 * NST, 1], F32, "ExternalInput")
        self.gamma = d("gamma", [128, DC], F32, "ExternalInput")
        self.beta = d("beta", [128, DC], F32, "ExternalInput")
        self.dskip = d("dskip", [128, DC], F32, "ExternalInput")
        self.ident = d("ident", [128, 128], F32, "ExternalInput")
        self.fconst = d("fconst", [128, 4], F32, "ExternalInput")

        self.y_out = d("y_fm", [128, DC, T], F32, "ExternalOutput")

        self.h_gate = d("h_gate", [128, IC, T], F32)
        self.h_up = d("h_up", [128, IC, T], F32)
        self.h_p = d("h_p", [128, IC, T], F32)
        self.h_o = d("h_o", [128, IC, T], F32)
        self.h_wgD = d("h_wgD", [128, DC, I], BF16)
        self.h_wuD = d("h_wuD", [128, DC, I], BF16)
        self.h_wdD = d("h_wdD", [128, IC, D], BF16)
        self.h_rg = d("h_rg", [32, I], F32)
        self.h_sg = d("h_sg", [32, I], F32)
        self.h_ru = d("h_ru", [32, I], F32)
        self.h_su = d("h_su", [32, I], F32)
        self.h_rd = d("h_rd", [96, D], F32)
        self.h_sd = d("h_sd", [96, D], F32)
        self.h_vec = d("h_vec", [8, T], F32)
        self.h_scal = d("h_scal", [1, NSCAL], F32)
        self.h_col = d("h_col", [128, 1], F32)
        self.h_bc = d("h_bc", [2 * NST, T], F32)
        self.h_delta = d("h_delta", [128, DC, T], F32)
        self.h_xn = d("h_xn", [128, DC, T], F32)
        self.h_dx = d("h_dx", [128, DC, T], F32)
        self.cc_in = [self._dram(f"cc_in{i}", [1, 2], F32) for i in range(8)]
        self.cc_out = [self._dram(f"cc_out{i}", [1, 2], F32) for i in range(8)]

    def _mk_global_sbuf(self):
        self.xs = self.sbuf("xs", [128, DC, T])
        self.cb = [self.sbuf(f"cb{i}", [128, T]) for i in range(3)]
        self.cb16 = [self.sbuf(f"cb16_{i}", [128, T], BF16) for i in range(2)]
        self.ob = [self.sbuf(f"ob{i}", [128, T]) for i in range(2)]
        self.acols = self.sbuf("acols", [128, DC * NST])
        self.biases = self.sbuf("biases", [128, DC])
        self.gb = self.sbuf("gb", [128, 3 * DC])
        self.bcb = self.sbuf("bcb", [2 * NST, 1])
        self.vrt = self.sbuf("vrt", [128, 16 * 10])
        self.vst = self.sbuf("vst", [1, T])
        self.scal = self.sbuf("scal", [128, NSCAL])
        self.sc1 = self.sbuf("sc1", [1, NSCAL])
        self.mx = self.sbuf("mx", [128, 32])
        self.idn = self.sbuf("idn", [128, 128])
        self.fcs = self.sbuf("fcs", [128, 4])
        self.bcast = [self.sbuf(f"bcast{i}", [128, T]) for i in range(2)]
        self.barsc = self.sbuf("barsc", [1, 4])
        ps = lambda name: self.ctx.enter_context(
            self.nc.psum_tensor(name, [128, TQ], F32))
        self.pmm = [ps(f"pmm{i}") for i in range(8)]

    def slot(self, name, eng):
        if name not in self._slots:
            sem = Sem(self.ctx, self.nc, f"sl_{name}")
            sem.eng = eng
            self._slots[name] = sem
        return self._slots[name]

    def _mk_sems(self):
        self._slots = {}
        c, n = self.ctx, self.nc
        self.bar = Sem(c, n, "bar")
        self.sd = Sem(c, n, "sd")
        self.gd = Sem(c, n, "gd")
        self.sv = Sem(c, n, "sv")
        self.ss = Sem(c, n, "ss")
        self.st = Sem(c, n, "st")
        self.sg = Sem(c, n, "sg")

    def emit(self, engine, fn):
        self.ops[engine].append(fn)

    def barrier(self):
        sdn, gdn = self.sd.n, self.gd.n

        def _dr_s(sync, sdn=sdn):
            sync.wait_ge(self.sd.h, sdn)

        def _dr_g(gpsimd, gdn=gdn):
            gpsimd.wait_ge(self.gd.h, gdn)
        self.ops["sync"].append(_dr_s)
        self.ops["gpsimd"].append(_dr_g)
        for sl in self._slots.values():
            if sl.n:
                n = sl.n

                def _dr(eng, sl=sl, n=n):
                    eng.wait_ge(sl.h, n)
                self.ops[sl.eng].append(_dr)
        self.phase_marks.append({e: len(self.ops[e]) for e in self.ops})

    def finalize(self):
        nc = self.nc
        marks = self.phase_marks
        nbar = len(marks)

        def bar_inc(eng_name, eng):
            if eng_name in ("vector", "gpsimd"):
                eng.engine_nop().then_inc(self.bar.h, 1)
            elif eng_name == "scalar":
                eng.wait_ge(self.ss.h, self._barsc_init)
                eng.copy(out=self.barsc[0:1, 0:1],
                         in_=self.barsc[0:1, 1:2]).then_inc(self.bar.h, 1)
            else:
                eng.sem_inc(self.bar.h, 1)

        def run(eng_name, eng):
            mi = 0
            for i, fn in enumerate(self.ops[eng_name]):
                while mi < nbar and marks[mi][eng_name] == i:
                    bar_inc(eng_name, eng)
                    eng.wait_ge(self.bar.h, 5 * (mi + 1))
                    mi += 1
                fn(eng)
            while mi < nbar:
                bar_inc(eng_name, eng)
                eng.wait_ge(self.bar.h, 5 * (mi + 1))
                mi += 1

        block = self.ctx.enter_context(nc.Block())

        @block.sync
        def _(e):
            run("sync", e)

        @block.vector
        def _(e):
            run("vector", e)

        @block.scalar
        def _(e):
            run("scalar", e)

        @block.tensor
        def _(e):
            run("tensor", e)

        @block.gpsimd
        def _(e):
            run("gpsimd", e)

        while self.scopes:
            self.pop_scope()
        self.ctx.close()
        return nc



def bcast_scalar(P, col):
    sd = P.sd

    def f1(sync, col=col):
        sync.dma_start(out=P.h_scal[0:1, col:col + 1],
                       in_=P.sc1[0:1, col:col + 1]).then_inc(sd.h, 16)
        sd.add(0)
    sd.add(16)
    P.emit("sync", f1)
    need = sd.n

    def f2(sync, col=col, need=need):
        sync.wait_ge(sd.h, need)
        src = P.h_scal[0:1, col:col + 1].broadcast_to([128, 1])
        sync.dma_start(out=P.scal[:, col:col + 1], in_=src).then_inc(sd.h, 16)
    sd.add(16)
    P.emit("sync", f2)


def finalize_mx(P, ncols, sc1_col):
    sd, sv = P.sd, P.sv

    svprev = sv.n

    def f0(vector, svprev=svprev):
        vector.wait_ge(P.sv.h, svprev)
        vector.tensor_reduce(out=P.mx[:, 31:32], in_=P.mx[:, 0:ncols],
                             axis=AX.X, op=ALU.max).then_inc(P.sv.h, 1)
    sv.add(1)
    P.emit("vector", f0)
    vneed = sv.n

    def f1(sync, vneed=vneed):
        sync.wait_ge(P.sv.h, vneed)
        sync.dma_start(out=P.h_col, in_=P.mx[:, 31:32]).then_inc(sd.h, 16)
    sd.add(16)
    P.emit("sync", f1)
    need = sd.n

    def f2(sync, need=need):
        sync.wait_ge(sd.h, need)
        sync.dma_start(out=P.vst[0:1, 0:128],
                       in_=P.h_col.rearrange("p c -> c p")).then_inc(sd.h, 16)
    sd.add(16)
    P.emit("sync", f2)
    need2 = sd.n

    sv.add(1)
    nf3 = sv.n

    def f3(vector, need2=need2, nf3=nf3):
        vector.wait_ge(sd.h, need2)
        vector.tensor_reduce(out=P.sc1[0:1, sc1_col:sc1_col + 1],
                             in_=P.vst[0:1, 0:128], axis=AX.X,
                             op=ALU.max).then_inc(P.sv.h, 1)
        vector.wait_ge(P.sv.h, nf3)
    P.emit("vector", f3)
    P.barrier()


def collective_max(P, sc1_cols):
    idx = P.cc_idx
    P.cc_idx += 1
    nv = len(sc1_cols)
    sd, sg = P.sd, P.sg
    assert nv <= 2
    for j, c in enumerate(sc1_cols):
        def fc(vector, j=j, c=c):
            vector.tensor_copy(out=P.sc1[0:1, S_T0 + j:S_T0 + j + 1],
                               in_=P.sc1[0:1, c:c + 1])
        P.emit("vector", fc)
    P.barrier()

    gd0, sg0 = P.gd.n, P.sg.n

    def f1(gpsimd, idx=idx, nv=nv, gd0=gd0, sg0=sg0):
        gpsimd.dma_start(out=P.cc_in[idx][0:1, 0:2],
                         in_=P.sc1[0:1, S_T0:S_T0 + 2]).then_inc(P.gd.h, 16)
        gpsimd.wait_ge(P.gd.h, gd0 + 16)
        gpsimd.collective_compute(
            "AllReduce", ALU.max, replica_groups=[list(range(NCORES))],
            ins=[P.cc_in[idx]], outs=[P.cc_out[idx]]).then_inc(P.sg.h, 1)
        gpsimd.wait_ge(P.sg.h, sg0 + 1)
        gpsimd.dma_start(out=P.sc1[0:1, S_T0:S_T0 + nv],
                         in_=P.cc_out[idx][0:1, 0:nv]).then_inc(P.gd.h, 16)
        gpsimd.wait_ge(P.gd.h, gd0 + 32)
    P.gd.add(32)
    P.sg.add(1)
    P.emit("gpsimd", f1)
    P.barrier()
    for j, c in enumerate(sc1_cols):
        def fc2(vector, j=j, c=c):
            vector.tensor_copy(out=P.sc1[0:1, c:c + 1],
                               in_=P.sc1[0:1, S_T0 + j:S_T0 + j + 1])
        P.emit("vector", fc2)
    P.barrier()


def qdq_scale_from_max(P, mcol, scol, rcol):
    assert mcol != scol and scol != rcol

    eq = edge(P, "sv")

    def f(vector, eq=eq):
        eq(vector, vector.tensor_scalar(out=P.sc1[0:1, scol:scol + 1],
                                        in0=P.sc1[0:1, mcol:mcol + 1],
                                        scalar1=1.0 / 32767.0, scalar2=1e-12,
                                        op0=ALU.mult, op1=ALU.max))
        vector.reciprocal(out=P.sc1[0:1, rcol:rcol + 1],
                          in_=P.sc1[0:1, scol:scol + 1])
    P.emit("vector", f)
    P.barrier()
    bcast_scalar(P, scol)
    bcast_scalar(P, rcol)
    P.barrier()


def bcast_row(P, hrow_ap, dst):
    if not isinstance(dst, bass.AP):
        dst = dst[:]
    W = dst.shape[-1]

    def f(gpsimd):
        src = hrow_ap.broadcast_to([128, W])
        gpsimd.dma_start(out=dst, in_=src).then_inc(P.gd.h, 16)
    P.gd.add(16)
    P.emit("gpsimd", f)



def edge(P, which):
    sem = getattr(P, which)
    sem.add(1)
    n = sem.n

    def apply(eng, ins, sem=sem, n=n):
        ins.then_inc(sem.h, 1)
        eng.wait_ge(sem.h, n)
    return apply


def vseg(P, k):
    return P.vrt[:, 16 * k:16 * (k + 1)]


def hrow_tiled(P, r):
    return P.h_vec[r:r + 1, :].rearrange("o (p f) -> (o p) f", p=128)


def vseg_to_hrow(P, k, r):
    sd = P.sd

    def f(sync, k=k, r=r):
        sync.dma_start(out=hrow_tiled(P, r), in_=vseg(P, k)).then_inc(sd.h, 16)
    sd.add(16)
    P.emit("sync", f)


def hrow_to_vseg(P, r, k):
    sd = P.sd

    def f(sync, k=k, r=r):
        sync.dma_start(out=vseg(P, k), in_=hrow_tiled(P, r)).then_inc(sd.h, 16)
    sd.add(16)
    P.emit("sync", f)



def ph_load(P):
    sd = P.sd

    def f_bs(scalar):
        scalar.memzero(P.barsc[:]).then_inc(P.ss.h, 1)
    P.ss.add(1)
    P.emit("scalar", f_bs)
    P._barsc_init = P.ss.n

    def f(sync):
        sync.dma_start(out=P.xs[:], in_=P.x_in).then_inc(sd.h, 16)
        sync.dma_start(out=P.acols[:], in_=P.alog).then_inc(sd.h, 16)
        sync.dma_start(out=P.biases[:], in_=P.bdelta).then_inc(sd.h, 16)
        sync.dma_start(out=P.gb[:, 0:DC], in_=P.gamma).then_inc(sd.h, 16)
        sync.dma_start(out=P.gb[:, DC:2 * DC], in_=P.beta).then_inc(sd.h, 16)
        sync.dma_start(out=P.gb[:, 2 * DC:3 * DC], in_=P.dskip).then_inc(sd.h, 16)
        sync.dma_start(out=P.idn[:], in_=P.ident).then_inc(sd.h, 16)
        sync.dma_start(out=P.fcs[:], in_=P.fconst).then_inc(sd.h, 16)
        sync.dma_start(out=P.bcb[:], in_=P.bBC).then_inc(sd.h, 16)
    sd.add(144)
    P.emit("sync", f)
    P.barrier()

    def f2(scalar):
        scalar.activation(out=P.acols[:], in_=P.acols[:], func=AF.Exp)
    P.emit("scalar", f2)
    P.barrier()

    em = edge(P, "sv")

    def f3(vector, em=em):
        vector.tensor_scalar(out=P.acols[:], in0=P.acols[:], scalar1=-1.0,
                             scalar2=None, op0=ALU.mult)
        em(vector, vector.memset(P.scal[:], 0.0))
        vector.memset(P.sc1[:], 0.0)
        vector.memset(P.vst[:], 0.0)
        vector.memset(P.vrt[:], 0.0)
        vector.memset(P.scal[:, S_ONES_:S_ONES_ + 1], 1.0)
        vector.memset(P.mx[:], 0.0)
    P.emit("vector", f3)
    P.barrier()


def ph_dequant(P, which):
    nc = P.nc
    if which == "g":
        wN, wT, hD, hR, hS = P.wgN, P.wgT, P.h_wgD, P.h_rg, P.h_sg
        C, IN, NB, CT, OF = IC, D, D // 32, DC, I
    elif which == "u":
        wN, wT, hD, hR, hS = P.wuN, P.wuT, P.h_wuD, P.h_ru, P.h_su
        C, IN, NB, CT, OF = IC, D, D // 32, DC, I
    else:
        wN, wT, hD, hR, hS = P.wdN, P.wdT, P.h_wdD, P.h_rd, P.h_sd
        C, IN, NB, CT, OF = DC, I, I // 32, IC, D

    sd, sv, ss, st, gd = P.sd, P.sv, P.ss, P.st, P.gd
    P.push_scope()
    dq = [P.sbuf(f"dq0_{which}", [128, 3072], scoped=True)]
    dq.append(dq[0])
    dq16 = P.sbuf(f"dq16_{which}", [128, 3072], BF16, scoped=True)
    dqr = P.sbuf(f"dqr_{which}", [128, 3072], scoped=True)
    dqs = P.sbuf(f"dqs_{which}", [128, 3072], scoped=True)
    qi16 = P.sbuf(f"qi16_{which}", [128, 3072], I16, scoped=True)
    srow_full = P.sbuf(f"srow_{which}", [96, 3072], scoped=True)
    srow = srow_full[0:NB, 0:C * 128]
    blkmax = P.ob[0]

    for c in range(C):
        buf = dq[c % 2]
        if c >= 1:
            vneed = sv.n

            def fw(sync, vneed=vneed):
                sync.wait_ge(P.sv.h, vneed)
            P.emit("sync", fw)

        sl = P.slot("dqn", "sync")
        sl.add(16)
        need = sl.n

        def f_load(sync, c=c, buf=buf, sl=sl):
            sync.dma_start(out=buf[:, 0:IN], in_=wN[:, c]).then_inc(sl.h, 16)
        P.emit("sync", f_load)

        def f_max(vector, c=c, buf=buf, need=need, sl=sl):
            vector.wait_ge(sl.h, need)
            w = buf[:, 0:IN].rearrange("p (b e) -> p b e", e=32)
            vector.tensor_reduce(out=blkmax[:, c * NB:(c + 1) * NB], in_=w,
                                 axis=AX.X, op=ALU.max,
                                 apply_absolute_value=True).then_inc(P.sv.h, 1)
        sv.add(1)
        P.emit("vector", f_max)
    P.barrier()

    for c in range(C):
        if c >= 2:
            vneed = sv.n - 1

            def fw2(tensor, vneed=vneed):
                tensor.wait_ge(P.sv.h, vneed)
            P.emit("tensor", fw2)

        def f_tr(tensor, c=c):
            tensor.transpose(out=P.pmm[c % 2][0:NB, 0:128],
                             in_=blkmax[:, c * NB:(c + 1) * NB],
                             identity=P.idn[:]).then_inc(P.st.h, 1)
        st.add(1)
        P.emit("tensor", f_tr)
        tneed = st.n

        def f_ev(vector, c=c, tneed=tneed):
            vector.wait_ge(P.st.h, tneed)
            vector.tensor_scalar(out=srow[0:NB, c * 128:(c + 1) * 128],
                                 in0=P.pmm[c % 2][0:NB, 0:128],
                                 scalar1=1.0 / 7.0, scalar2=1e-12,
                                 op0=ALU.mult, op1=ALU.max).then_inc(P.sv.h, 1)
        sv.add(1)
        P.emit("vector", f_ev)
    P.barrier()

    def f_ws(sync):
        sync.dma_start(out=hS, in_=srow).then_inc(sd.h, 16)
    sd.add(16)
    P.emit("sync", f_ws)
    P.barrier()

    def f_rec(vector):
        W = C * 128
        for j0 in range(0, W, 512):
            j1 = min(j0 + 512, W)
            vector.reciprocal(out=dqs[0:NB, j0:j1], in_=srow_full[0:NB, j0:j1])
    P.emit("vector", f_rec)
    P.barrier()

    def f_wr(sync):
        sync.dma_start(out=hR, in_=dqs[0:NB, 0:C * 128]).then_inc(sd.h, 16)
    sd.add(16)
    P.emit("sync", f_wr)
    P.barrier()

    rbc, sbc = dqr, dqs
    for ct in range(CT):
        b0 = 4 * ct

        sla = P.slot("dqa", "sync")
        slb = P.slot("dqb", "gpsimd")
        sla.add(16)
        slb.add(128)

        def f_ld(sync, ct=ct, sla=sla):
            sync.dma_start(out=dq[0][:, 0:OF], in_=wT[:, ct]).then_inc(sla.h, 16)
        P.emit("sync", f_ld)

        def f_bc(gpsimd, b0=b0, slb=slb):
            for j in range(4):
                gpsimd.dma_start(out=rbc[32 * j:32 * (j + 1), 0:OF],
                                 in_=hR[b0 + j:b0 + j + 1, :].broadcast_to([32, OF])
                                 ).then_inc(slb.h, 16)
                gpsimd.dma_start(out=sbc[32 * j:32 * (j + 1), 0:OF],
                                 in_=hS[b0 + j:b0 + j + 1, :].broadcast_to([32, OF])
                                 ).then_inc(slb.h, 16)
        P.emit("gpsimd", f_bc)
        sneed, gneed = sla.n, slb.n

        e1, e2, e4, e5, e6 = (edge(P, "sv") for _ in range(5))

        def f_q(vector, sneed=sneed, gneed=gneed, sla=sla, slb=slb,
                e1=e1, e2=e2, e4=e4, e5=e5, e6=e6):
            vector.wait_ge(sla.h, sneed)
            vector.wait_ge(slb.h, gneed)
            buf = dq[0][:, 0:OF]
            e1(vector, vector.tensor_mul(out=buf, in0=buf, in1=rbc[:, 0:OF]))
            e2(vector, vector.tensor_scalar(out=rbc[:, 0:OF], in0=buf,
                                            scalar1=MAGIC, scalar2=None,
                                            op0=ALU.add))
            e4(vector, vector.tensor_scalar(out=buf, in0=rbc[:, 0:OF],
                                            scalar1=MAGIC, scalar2=None,
                                            op0=ALU.subtract))
            e5(vector, vector.tensor_copy(out=qi16[:, 0:OF], in_=buf))
            e6(vector, vector.tensor_scalar(out=buf, in0=qi16[:, 0:OF],
                                            scalar1=1.0, scalar2=None,
                                            op0=ALU.mult))
            vector.tensor_mul(out=rbc[:, 0:OF], in0=buf,
                              in1=sbc[:, 0:OF]).then_inc(P.sv.h, 1)
        sv.add(1)
        P.emit("vector", f_q)
        sv.add(1)
        vneed = sv.n

        def f_cp(vector, vneed=vneed):
            vector.wait_ge(P.sv.h, vneed - 1)
            vector.tensor_copy(out=dq16[:, 0:OF],
                               in_=rbc[:, 0:OF]).then_inc(P.sv.h, 1)
        P.emit("vector", f_cp)

        def f_st(sync, ct=ct, vneed=vneed):
            sync.wait_ge(P.sv.h, vneed)
            sync.dma_start(out=hD[:, ct], in_=dq16[:, 0:OF]).then_inc(sd.h, 16)
        sd.add(16)
        P.emit("sync", f_st)
        P.barrier()
    P.pop_scope()


def ph_ln_stats(P, src, sum_seg, sq_seg):
    st, ss, sv, sd = P.st, P.ss, P.sv, P.sd
    ones = P.scal[:, S_ONES_:S_ONES_ + 1]
    sq = P.ob[1]

    for c in range(DC):
        if c >= 1:
            tprev = st.n

            def fw(scalar, tprev=tprev):
                scalar.wait_ge(P.st.h, tprev)
            P.emit("scalar", fw)

        def f_sq(scalar, c=c):
            scalar.activation(out=sq[:], in_=src[:, c], func=AF.Square).then_inc(P.ss.h, 1)
        ss.add(1)
        P.emit("scalar", f_sq)
        sneed = ss.n

        def f_mm(tensor, c=c, sneed=sneed):
            tensor.wait_ge(P.ss.h, sneed)
            last = None
            for tt in range(TT4):
                tensor.matmul(P.pmm[tt][0:1, :], ones,
                              src[:, c, tt * TQ:(tt + 1) * TQ],
                              start=(c == 0), stop=(c == DC - 1))
                last = tensor.matmul(P.pmm[4 + tt][0:1, :], ones,
                                     sq[:, tt * TQ:(tt + 1) * TQ],
                                     start=(c == 0), stop=(c == DC - 1))
            last.then_inc(P.st.h, 1)
        st.add(1)
        P.emit("tensor", f_mm)
    tneed = st.n
    ph_stats_evac(P, tneed, sum_seg, sq_seg)


def ph_stats_evac(P, tneed, sum_seg, sq_seg):
    sv, sd = P.sv, P.sd

    def f_ev(vector, tneed=tneed):
        vector.wait_ge(P.st.h, tneed)
        for tt in range(TT4):
            vector.tensor_copy(out=P.vst[0:1, tt * TQ:(tt + 1) * TQ],
                               in_=P.pmm[tt][0:1, :])
        vector.engine_nop().then_inc(P.sv.h, 1)
    sv.add(1)
    P.emit("vector", f_ev)
    vneed = sv.n

    def f_w(sync, vneed=vneed):
        sync.wait_ge(P.sv.h, vneed)
        sync.dma_start(out=P.h_vec[6:7, :], in_=P.vst[0:1, :]).then_inc(sd.h, 16)
    sd.add(16)
    P.emit("sync", f_w)
    P.barrier()

    def f_ev2(vector):
        for tt in range(TT4):
            vector.tensor_copy(out=P.vst[0:1, tt * TQ:(tt + 1) * TQ],
                               in_=P.pmm[4 + tt][0:1, :])
        vector.engine_nop().then_inc(P.sv.h, 1)
    sv.add(1)
    P.emit("vector", f_ev2)
    vneed2 = sv.n

    def f_w2(sync, vneed2=vneed2):
        sync.wait_ge(P.sv.h, vneed2)
        sync.dma_start(out=P.h_vec[7:8, :], in_=P.vst[0:1, :]).then_inc(sd.h, 16)
    sd.add(16)
    P.emit("sync", f_w2)
    P.barrier()
    hrow_to_vseg(P, 6, sum_seg)
    hrow_to_vseg(P, 7, sq_seg)
    P.barrier()


def ph_mu_rho(P, sum_seg, sq_seg, mu_seg, rho_seg):
    EX2, MSQ, VAR = 4, 5, 6

    ev = [edge(P, "sv") for _ in range(3)]

    def fv(vector, ev=ev):
        ev[0](vector, vector.tensor_scalar(out=vseg(P, mu_seg),
                                           in0=vseg(P, sum_seg),
                                           scalar1=1.0 / D, scalar2=None,
                                           op0=ALU.mult))
        ev[1](vector, vector.tensor_scalar(out=vseg(P, EX2), in0=vseg(P, sq_seg),
                                           scalar1=1.0 / D, scalar2=None,
                                           op0=ALU.mult))
        ev[2](vector, vector.tensor_mul(out=vseg(P, MSQ), in0=vseg(P, mu_seg),
                                        in1=vseg(P, mu_seg)))
        vector.tensor_sub(out=vseg(P, VAR), in0=vseg(P, EX2),
                          in1=vseg(P, MSQ)).then_inc(P.sv.h, 1)
    P.sv.add(1)
    P.emit("vector", fv)
    vneed = P.sv.n
    e2 = edge(P, "ss")

    def fs(scalar, vneed=vneed, e2=e2):
        scalar.wait_ge(P.sv.h, vneed)
        e2(scalar, scalar.activation(out=vseg(P, EX2), in_=vseg(P, VAR),
                                     func=AF.Ln, bias=P.fcs[:, 2:3]))
        scalar.activation(out=vseg(P, rho_seg), in_=vseg(P, EX2), func=AF.Exp,
                          scale=P.fcs[:, 3:4]).then_inc(P.ss.h, 1)
    P.ss.add(1)
    P.emit("scalar", fs)
    P.barrier()


def ph_vec_to_hbm_and_bcast(P, segs_rows_dsts):
    for seg, r, dst in segs_rows_dsts:
        vseg_to_hrow(P, seg, r)
    P.barrier()
    for seg, r, dst in segs_rows_dsts:
        bcast_row(P, P.h_vec[r:r + 1, :], dst)
    P.barrier()



def ph_ffn_xq(P, src, xq):
    MU, RHO, PR, QR = 2, 3, 7, 8
    def f_pq(vector):
        vector.tensor_mul(out=vseg(P, QR), in0=vseg(P, MU), in1=vseg(P, RHO))
        vector.tensor_copy(out=vseg(P, PR), in_=vseg(P, RHO))
    P.emit("vector", f_pq)
    P.barrier()
    ph_vec_to_hbm_and_bcast(P, [(PR, 0, P.bcast[0]), (QR, 1, P.bcast[1])])

    edges_mx = [(edge(P, "sv"), edge(P, "sv"), edge(P, "sv")) for _ in range(DC)]

    def f_mx(vector, edges=edges_mx):
        for c in range(DC):
            ea, eb, ec = edges[c]
            ea(vector, vector.tensor_mul(out=P.ob[0][:], in0=src[:, c],
                                         in1=P.bcast[0][:]))
            eb(vector, vector.tensor_sub(out=P.ob[0][:], in0=P.ob[0][:],
                                         in1=P.bcast[1][:]))
            ec(vector, vector.tensor_reduce(out=P.mx[:, c:c + 1], in_=P.ob[0][:],
                                            axis=AX.X, op=ALU.max,
                                            apply_absolute_value=True))
    P.emit("vector", f_mx)
    P.barrier()
    finalize_mx(P, DC, M_X)
    collective_max(P, [M_X])
    qdq_scale_from_max(P, M_X, S_SX, S_RX)

    ep = edge(P, "sv")

    def f_prow(vector, ep=ep):
        ep(vector, vector.tensor_scalar(out=vseg(P, PR), in0=vseg(P, RHO),
                                        scalar1=P.scal[:, S_RX:S_RX + 1],
                                        scalar2=None, op0=ALU.mult))
        vector.tensor_mul(out=vseg(P, QR), in0=vseg(P, MU), in1=vseg(P, PR))
    P.emit("vector", f_prow)
    P.barrier()
    ph_vec_to_hbm_and_bcast(P, [(PR, 0, P.bcast[0]), (QR, 1, P.bcast[1])])

    edges_q = [(edge(P, "sv"), edge(P, "sv"), edge(P, "sv"), edge(P, "sv"))
               for _ in range(DC)]

    def f_q(vector, edges=edges_q):
        for c in range(DC):
            ea, eb, ec, ed = edges[c]
            ea(vector, vector.tensor_mul(out=P.ob[0][:], in0=src[:, c],
                                         in1=P.bcast[0][:]))
            eb(vector, vector.tensor_sub(out=P.ob[0][:], in0=P.ob[0][:],
                                         in1=P.bcast[1][:]))
            ec(vector, vector.tensor_scalar(out=P.ob[0][:], in0=P.ob[0][:],
                                            scalar1=MAGIC, scalar2=MAGIC,
                                            op0=ALU.add, op1=ALU.subtract))
            ed(vector, vector.tensor_scalar(out=xq[:, c], in0=P.ob[0][:],
                                            scalar1=P.scal[:, S_SX:S_SX + 1],
                                            scalar2=None, op0=ALU.mult))
    P.emit("vector", f_q)
    P.barrier()


def ph_ffn_mm_up(P, xq, wD_hbm, spill, mcol, wst):
    sd, st, ss, sv = P.sd, P.st, P.ss, P.sv

    evac_done = []
    red_done = []
    spill_done = []
    slw = P.slot(f"fwl_{mcol}", "sync")
    sp0 = P.slot(f"fsp0_{mcol}", "gpsimd")
    sp1 = P.slot(f"fsp1_{mcol}", "gpsimd")
    sps = [sp0, sp1]
    for half in range(2):
        i0 = half * (IC // 2)
        slw.add(16)
        sdn = slw.n
        if half == 1:
            ssprev = ss.n

            def fw_h(sync, ssprev=ssprev):
                sync.wait_ge(P.ss.h, ssprev)
            P.emit("sync", fw_h)

        def f_ldw(sync, half=half, sdn=sdn, slw=slw):
            sync.dma_start(out=wst[:],
                           in_=wD_hbm[:, :, half * 1536:(half + 1) * 1536]
                           ).then_inc(slw.h, 16)
            sync.wait_ge(slw.h, sdn)
        P.emit("sync", f_ldw)
        if half == 0:
            P.barrier()
        else:
            def fw_t(tensor, sdn=sdn, slw=slw):
                tensor.wait_ge(slw.h, sdn)
            P.emit("tensor", fw_t)

        for ii in range(IC // 2):
            i = i0 + ii
            grp = 4 * (i % 2)
            if i >= 2:
                need = evac_done[i - 2]

                def fwp(tensor, need=need):
                    tensor.wait_ge(P.ss.h, need)
                P.emit("tensor", fwp)

            def f_mm(tensor, ii=ii, grp=grp):
                last = None
                for k in range(DC):
                    stat = wst[:, k, 128 * ii:128 * (ii + 1)]
                    for tt in range(TT4):
                        last = tensor.matmul(P.pmm[grp + tt][:],
                                             stat, xq[:, k, tt * TQ:(tt + 1) * TQ],
                                             start=(k == 0), stop=(k == DC - 1))
                last.then_inc(P.st.h, 1)
            st.add(1)
            P.emit("tensor", f_mm)
            tneed = st.n
            buf = P.cb[i % 2]

            def f_ev(scalar, i=i, grp=grp, tneed=tneed, buf=buf,
                     sl=sps[i % 2], gneed=sps[i % 2].n,
                     vneed=(red_done[i - 2] if i >= 2 else 0)):
                scalar.wait_ge(P.st.h, tneed)
                if gneed:
                    scalar.wait_ge(sl.h, gneed)
                if vneed:
                    scalar.wait_ge(P.sv.h, vneed)
                last = None
                for tt in range(TT4):
                    last = scalar.copy(out=buf[:, tt * TQ:(tt + 1) * TQ],
                                       in_=P.pmm[grp + tt][:])
                last.then_inc(P.ss.h, 1)
            ss.add(1)
            P.emit("scalar", f_ev)
            evac_done.append(ss.n)
            sneed = ss.n

            def f_red(vector, i=i, buf=buf, sneed=sneed):
                vector.wait_ge(P.ss.h, sneed)
                vector.tensor_reduce(out=P.mx[:, i:i + 1], in_=buf[:], axis=AX.X,
                                     op=ALU.max,
                                     apply_absolute_value=True).then_inc(P.sv.h, 1)
            sv.add(1)
            P.emit("vector", f_red)
            red_done.append(sv.n)

            sps[i % 2].add(16)

            def f_sp(gpsimd, i=i, buf=buf, sneed=sneed, sl=sps[i % 2]):
                gpsimd.wait_ge(P.ss.h, sneed)
                gpsimd.dma_start(out=spill[:, i], in_=buf[:]).then_inc(sl.h, 16)
            P.emit("gpsimd", f_sp)
            spill_done.append(0)
    P.barrier()
    finalize_mx(P, IC, mcol)


def ph_pchain(P):
    sd, sv, ss = P.sd, P.sv, P.ss
    def f_m(vector):
        vector.tensor_scalar(out=P.sc1[0:1, S_MSGM:S_MSGM + 1],
                             in0=P.sc1[0:1, S_SG:S_SG + 1],
                             scalar1=-MAGIC, scalar2=None, op0=ALU.mult)
    P.emit("vector", f_m)
    P.barrier()
    bcast_scalar(P, S_MSGM)
    P.barrier()

    pl = [P.slot("pl0", "sync"), P.slot("pl1", "sync")]
    psp = [P.slot("psp0", "gpsimd"), P.slot("psp1", "gpsimd")]
    for i in range(IC):
        buf = P.cb[i % 2]
        sig = P.cb[2]
        gneed = psp[i % 2].n

        def fw(sync, gneed=gneed, sl=psp[i % 2]):
            if gneed:
                sync.wait_ge(sl.h, gneed)
        P.emit("sync", fw)
        pl[i % 2].add(16)
        dneed = pl[i % 2].n

        def f_ld(sync, i=i, buf=buf, sl=pl[i % 2]):
            sync.dma_start(out=buf[:], in_=P.h_gate[:, i]).then_inc(sl.h, 16)
        P.emit("sync", f_ld)

        def f_w1(vector, buf=buf, dneed=dneed, sl=pl[i % 2]):
            vector.wait_ge(sl.h, dneed)
            vector.tensor_scalar(out=buf[:], in0=buf[:],
                                 scalar1=P.scal[:, S_RG:S_RG + 1], scalar2=MAGIC,
                                 op0=ALU.mult, op1=ALU.add).then_inc(P.sv.h, 1)
        sv.add(1)
        P.emit("vector", f_w1)
        vneed = sv.n

        es = edge(P, "ss")

        def f_sig(scalar, buf=buf, sig=sig, vneed=vneed, es=es):
            scalar.wait_ge(P.sv.h, vneed)
            es(scalar, scalar.activation(out=sig[:], in_=buf[:], func=AF.Sigmoid,
                                         scale=P.scal[:, S_SG:S_SG + 1],
                                         bias=P.scal[:, S_MSGM:S_MSGM + 1]))
            scalar.activation(out=sig[:], in_=sig[:], func=AF.Identity,
                              scale=P.fcs[:, 0:1],
                              bias=P.fcs[:, 1:2]).then_inc(P.ss.h, 1)
        ss.add(1)
        P.emit("scalar", f_sig)
        sneed = ss.n

        ep0, ep1, ep2 = edge(P, "sv"), edge(P, "sv"), edge(P, "sv")

        def f_p(vector, i=i, buf=buf, sig=sig, sneed=sneed,
                ep0=ep0, ep1=ep1, ep2=ep2):
            vector.wait_ge(P.ss.h, sneed)
            ep0(vector, vector.tensor_scalar(out=buf[:], in0=buf[:], scalar1=MAGIC,
                                             scalar2=None, op0=ALU.subtract))
            ep1(vector, vector.tensor_scalar(out=sig[:], in0=sig[:], scalar1=MAGIC,
                                             scalar2=65535.0, op0=ALU.subtract,
                                             op1=ALU.min))
            ep2(vector, vector.tensor_mul(out=buf[:], in0=buf[:], in1=sig[:]))
            vector.tensor_reduce(out=P.mx[:, i:i + 1], in_=buf[:], axis=AX.X,
                                 op=ALU.max,
                                 apply_absolute_value=True).then_inc(P.sv.h, 1)
        sv.add(1)
        P.emit("vector", f_p)
        vneed2 = sv.n

        psp[i % 2].add(16)

        def f_sp(gpsimd, i=i, buf=buf, vneed2=vneed2, sl=psp[i % 2]):
            gpsimd.wait_ge(P.sv.h, vneed2)
            gpsimd.dma_start(out=P.h_p[:, i], in_=buf[:]).then_inc(sl.h, 16)
        P.emit("gpsimd", f_sp)
    P.barrier()
    finalize_mx(P, IC, M_P)


def ph_opass(P):
    sd, sv = P.sd, P.sv
    ol = [P.slot("ol0", "sync"), P.slot("ol1", "sync")]
    osp = [P.slot("osp0", "gpsimd"), P.slot("osp1", "gpsimd")]
    for i in range(IC):
        pb = P.cb[i % 2]
        ub = P.ob[i % 2]
        gneed = osp[i % 2].n

        def fw(sync, gneed=gneed, sl=osp[i % 2]):
            if gneed:
                sync.wait_ge(sl.h, gneed)
        P.emit("sync", fw)
        ol[i % 2].add(32)
        dneed = ol[i % 2].n

        def f_ld(sync, i=i, pb=pb, ub=ub, sl=ol[i % 2]):
            sync.dma_start(out=pb[:], in_=P.h_p[:, i]).then_inc(sl.h, 16)
            sync.dma_start(out=ub[:], in_=P.h_up[:, i]).then_inc(sl.h, 16)
        P.emit("sync", f_ld)

        eo = [edge(P, "sv") for _ in range(5)]

        def f_o(vector, i=i, pb=pb, ub=ub, dneed=dneed, sl=ol[i % 2], eo=eo):
            vector.wait_ge(sl.h, dneed)
            eo[0](vector, vector.tensor_scalar(out=pb[:], in0=pb[:],
                                               scalar1=P.scal[:, S_RPP:S_RPP + 1],
                                               scalar2=MAGIC,
                                               op0=ALU.mult, op1=ALU.add))
            eo[1](vector, vector.tensor_scalar(out=ub[:], in0=ub[:],
                                               scalar1=P.scal[:, S_RU:S_RU + 1],
                                               scalar2=MAGIC, op0=ALU.mult,
                                               op1=ALU.add))
            eo[2](vector, vector.tensor_scalar(out=pb[:], in0=pb[:], scalar1=MAGIC,
                                               scalar2=None, op0=ALU.subtract))
            eo[3](vector, vector.tensor_scalar(out=ub[:], in0=ub[:], scalar1=MAGIC,
                                               scalar2=None, op0=ALU.subtract))
            eo[4](vector, vector.tensor_mul(out=pb[:], in0=pb[:], in1=ub[:]))
            vector.tensor_reduce(out=P.mx[:, i:i + 1], in_=pb[:], axis=AX.X,
                                 op=ALU.max,
                                 apply_absolute_value=True).then_inc(P.sv.h, 1)
        sv.add(1)
        P.emit("vector", f_o)
        vneed2 = sv.n

        osp[i % 2].add(16)

        def f_sp(gpsimd, i=i, pb=pb, vneed2=vneed2, sl=osp[i % 2]):
            gpsimd.wait_ge(P.sv.h, vneed2)
            gpsimd.dma_start(out=P.h_o[:, i], in_=pb[:]).then_inc(sl.h, 16)
        P.emit("gpsimd", f_sp)
    P.barrier()
    finalize_mx(P, IC, M_O)


def ph_down(P, wst, dst_xs, write_out):
    sd, st, sv = P.sd, P.st, P.sv

    dwl = P.slot("dwl", "sync")
    dol = [P.slot("dol0", "sync"), P.slot("dol1", "sync")]
    for tt in range(TT4):
        conv_done = []
        mm_done = []
        for half in range(2):
            k0 = half * 12
            dwl.add(16)
            sdn = dwl.n
            tprev = st.n

            def f_ldw(sync, half=half, sdn=sdn, tprev=tprev, dwl=dwl):
                if tprev:
                    sync.wait_ge(P.st.h, tprev)
                sync.dma_start(out=wst[:],
                               in_=P.h_wdD[:, 12 * half:12 * (half + 1), :]
                               ).then_inc(dwl.h, 16)
            P.emit("sync", f_ldw)

            def fw_t(tensor, sdn=sdn, dwl=dwl):
                tensor.wait_ge(dwl.h, sdn)
            P.emit("tensor", fw_t)

            for kk in range(12):
                k = k0 + kk
                ob = P.ob[k % 2]
                qb = P.cb16[k % 2]
                if k >= 2:
                    vneed = conv_done[k - 2]

                    def fw(sync, vneed=vneed):
                        sync.wait_ge(P.sv.h, vneed)
                    P.emit("sync", fw)
                dol[k % 2].add(16)
                dneed = dol[k % 2].n

                def f_ld(sync, k=k, tt=tt, ob=ob, sl=dol[k % 2]):
                    sync.dma_start(out=ob[:, 0:TQ],
                                   in_=P.h_o[:, k, tt * TQ:(tt + 1) * TQ]).then_inc(sl.h, 16)
                P.emit("sync", f_ld)

                ed = edge(P, "sv")

                def f_q(vector, ob=ob, qb=qb, dneed=dneed, sl=dol[k % 2], ed=ed,
                        tneed=(mm_done[k - 2] if k >= 2 else 0)):
                    vector.wait_ge(sl.h, dneed)
                    if tneed:
                        vector.wait_ge(P.st.h, tneed)
                    ed(vector, vector.tensor_scalar(
                        out=ob[:, 0:TQ], in0=ob[:, 0:TQ],
                        scalar1=P.scal[:, S_ROP:S_ROP + 1],
                        scalar2=MAGIC, op0=ALU.mult, op1=ALU.add))
                    vector.tensor_scalar(out=qb[:, 0:TQ], in0=ob[:, 0:TQ],
                                         scalar1=MAGIC,
                                         scalar2=P.scal[:, S_SO:S_SO + 1],
                                         op0=ALU.subtract, op1=ALU.mult).then_inc(P.sv.h, 1)
                sv.add(1)
                P.emit("vector", f_q)
                conv_done.append(sv.n)
                vneed2 = sv.n

                def f_mm(tensor, kk=kk, k=k, qb=qb, vneed2=vneed2):
                    tensor.wait_ge(P.sv.h, vneed2)
                    last = None
                    for do in range(DC):
                        last = tensor.matmul(P.pmm[do][:],
                                             wst[:, kk, 128 * do:128 * (do + 1)],
                                             qb[:, 0:TQ], start=(k == 0),
                                             stop=(k == IC - 1))
                    last.then_inc(P.st.h, 1)
                st.add(1)
                P.emit("tensor", f_mm)
                mm_done.append(st.n)
        tneed = st.n

        def f_ev(vector, tt=tt, tneed=tneed):
            vector.wait_ge(P.st.h, tneed)
            for do in range(DC):
                dst = dst_xs[:, do, tt * TQ:(tt + 1) * TQ]
                vector.scalar_tensor_tensor(out=dst, in0=P.pmm[do][:], scalar=0.5,
                                            in1=P.xs[:, do, tt * TQ:(tt + 1) * TQ],
                                            op0=ALU.mult, op1=ALU.add)
            vector.engine_nop().then_inc(P.sv.h, 1)
        sv.add(1)
        P.emit("vector", f_ev)
        if write_out:
            vneed3 = sv.n

            def f_wo(sync, tt=tt, vneed3=vneed3):
                sync.wait_ge(P.sv.h, vneed3)
                for do in range(DC):
                    sync.dma_start(out=P.y_out[:, do, tt * TQ:(tt + 1) * TQ],
                                   in_=dst_xs[:, do, tt * TQ:(tt + 1) * TQ]
                                   ).then_inc(sd.h, 16)
            sd.add(16 * DC)
            P.emit("sync", f_wo)
        P.barrier()
    P.barrier()


def ph_ffn(P, call, stop=None):
    P.push_scope()
    xq = P.sbuf(f"xq{call}", [128, DC, T], BF16, scoped=True)
    wst = P.sbuf(f"wst{call}", [128, 12288], BF16, scoped=True)
    wst_gu = wst[:].rearrange("p (a b) -> p a b", a=DC)
    wst_d = wst[:].rearrange("p (a b) -> p a b", a=12)

    ph_ln_stats(P, P.xs, 0, 1)
    ph_mu_rho(P, 0, 1, 2, 3)
    if stop == "stats":
        P.pop_scope()
        return
    ph_ffn_xq(P, P.xs, xq)
    if stop == "xq":
        P.pop_scope()
        return
    ph_ffn_mm_up(P, xq, P.h_wgD, P.h_gate, M_G, wst_gu)
    if stop == "gate":
        P.pop_scope()
        return
    collective_max(P, [M_G])
    qdq_scale_from_max(P, M_G, S_SG, S_RG)
    if stop == "coll":
        P.pop_scope()
        return
    ph_pchain(P)
    if stop == "pchain":
        P.pop_scope()
        return
    ph_ffn_mm_up(P, xq, P.h_wuD, P.h_up, M_U, wst_gu)
    if stop == "up":
        P.pop_scope()
        return
    collective_max(P, [M_U, M_P])
    qdq_scale_from_max(P, M_U, S_SU, S_RU)

    efs = [edge(P, "sv") for _ in range(4)]

    def f_sp(vector, efs=efs):
        efs[0](vector, vector.tensor_scalar(out=P.sc1[0:1, S_K:S_K + 1],
                                            in0=P.sc1[0:1, S_SG:S_SG + 1],
                                            scalar1=1.0 / 65536.0, scalar2=None,
                                            op0=ALU.mult))
        efs[1](vector, vector.tensor_mul(out=P.sc1[0:1, S_TA:S_TA + 1],
                                         in0=P.sc1[0:1, M_P:M_P + 1],
                                         in1=P.sc1[0:1, S_K:S_K + 1]))
        efs[2](vector, vector.tensor_scalar(out=P.sc1[0:1, S_SP:S_SP + 1],
                                            in0=P.sc1[0:1, S_TA:S_TA + 1],
                                            scalar1=1.0 / 32767.0, scalar2=1e-12,
                                            op0=ALU.mult, op1=ALU.max))
        efs[3](vector, vector.reciprocal(out=P.sc1[0:1, S_TB:S_TB + 1],
                                         in_=P.sc1[0:1, S_SP:S_SP + 1]))
        vector.tensor_mul(out=P.sc1[0:1, S_RPP:S_RPP + 1],
                          in0=P.sc1[0:1, S_TB:S_TB + 1],
                          in1=P.sc1[0:1, S_K:S_K + 1])
    P.emit("vector", f_sp)
    P.barrier()
    bcast_scalar(P, S_RPP)
    bcast_scalar(P, S_RU)
    P.barrier()
    ph_opass(P)
    if stop == "opass":
        P.pop_scope()
        return

    eso = edge(P, "sv")

    def f_so(vector, eso=eso):
        eso(vector, vector.tensor_mul(out=P.sc1[0:1, S_TA:S_TA + 1],
                                      in0=P.sc1[0:1, M_O:M_O + 1],
                                      in1=P.sc1[0:1, S_SP:S_SP + 1]))
        vector.tensor_mul(out=P.sc1[0:1, M_O:M_O + 1],
                          in0=P.sc1[0:1, S_TA:S_TA + 1],
                          in1=P.sc1[0:1, S_SU:S_SU + 1])
    P.emit("vector", f_so)
    P.barrier()
    collective_max(P, [M_O])
    qdq_scale_from_max(P, M_O, S_SO, S_TC)
    erop = edge(P, "sv")

    def f_rop(vector, erop=erop):
        erop(vector, vector.tensor_mul(out=P.sc1[0:1, S_TA:S_TA + 1],
                                       in0=P.sc1[0:1, S_TC:S_TC + 1],
                                       in1=P.sc1[0:1, S_SP:S_SP + 1]))
        vector.tensor_mul(out=P.sc1[0:1, S_ROP:S_ROP + 1],
                          in0=P.sc1[0:1, S_TA:S_TA + 1],
                          in1=P.sc1[0:1, S_SU:S_SU + 1])
    P.emit("vector", f_rop)
    P.barrier()
    bcast_scalar(P, S_ROP)
    bcast_scalar(P, S_SO)
    P.barrier()
    ph_down(P, wst_d, P.xs, write_out=(call == 1))
    P.pop_scope()



def ph_attn_norms(P):
    sd, st, ss, sv = P.sd, P.st, P.ss, P.sv
    ones = P.scal[:, S_ONES_:S_ONES_ + 1]

    ph_ln_stats(P, P.xs, 0, 1)
    ph_mu_rho(P, 0, 1, 2, 3)

    def f_c1(vector):
        vector.tensor_mul(out=vseg(P, 9), in0=vseg(P, 2), in1=vseg(P, 3))
    P.emit("vector", f_c1)
    P.barrier()
    ph_vec_to_hbm_and_bcast(P, [(3, 0, P.bcast[0]), (9, 1, P.bcast[1])])

    ns = [P.slot("ns0", "gpsimd"), P.slot("ns1", "gpsimd")]
    mmst = []
    for c in range(DC):
        exl = [edge(P, "sv") for _ in range(2)]

        def f_xln(vector, c=c, sl=ns[c % 2], gprev=ns[c % 2].n, exl=exl,
                  tprev=(mmst[c - 2] if c >= 2 else 0)):
            if gprev:
                vector.wait_ge(sl.h, gprev)
            if tprev:
                vector.wait_ge(P.st.h, tprev)
            ob = P.ob[c % 2]
            exl[0](vector, vector.tensor_mul(out=ob[:], in0=P.xs[:, c],
                                             in1=P.bcast[0][:]))
            exl[1](vector, vector.tensor_sub(out=ob[:], in0=ob[:],
                                             in1=P.bcast[1][:]))
            vector.tensor_add(out=P.xs[:, c], in0=P.xs[:, c],
                              in1=ob[:]).then_inc(P.sv.h, 1)
        sv.add(1)
        P.emit("vector", f_xln)
        vneed = sv.n

        def f_sq(scalar, c=c, vneed=vneed, tprev=(st.n if c >= 1 else 0)):
            scalar.wait_ge(P.sv.h, vneed)
            if tprev:
                scalar.wait_ge(P.st.h, tprev)
            scalar.activation(out=P.cb[2][:], in_=P.ob[c % 2][:],
                              func=AF.Square).then_inc(P.ss.h, 1)
        ss.add(1)
        P.emit("scalar", f_sq)
        sneed = ss.n

        def f_mm(tensor, c=c, sneed=sneed):
            tensor.wait_ge(P.ss.h, sneed)
            last = None
            for tt in range(TT4):
                tensor.matmul(P.pmm[tt][0:1, :], ones,
                              P.ob[c % 2][:, tt * TQ:(tt + 1) * TQ],
                              start=(c == 0), stop=(c == DC - 1))
                last = tensor.matmul(P.pmm[4 + tt][0:1, :], ones,
                                     P.cb[2][:, tt * TQ:(tt + 1) * TQ],
                                     start=(c == 0), stop=(c == DC - 1))
            last.then_inc(P.st.h, 1)
        st.add(1)
        P.emit("tensor", f_mm)
        mmst.append(st.n)

        ns[c % 2].add(16)

        def f_sp(gpsimd, c=c, sneed=sneed, sl=ns[c % 2]):
            gpsimd.wait_ge(P.ss.h, sneed)
            gpsimd.dma_start(out=P.h_xn[:, c], in_=P.ob[c % 2][:]).then_inc(sl.h, 16)
        P.emit("gpsimd", f_sp)
    tneed = st.n
    ph_stats_evac(P, tneed, 0, 1)
    ph_mu_rho(P, 0, 1, 2, 3)

    def f_c2(vector):
        vector.tensor_mul(out=vseg(P, 9), in0=vseg(P, 2), in1=vseg(P, 3))
    P.emit("vector", f_c2)
    P.barrier()
    ph_vec_to_hbm_and_bcast(P, [(3, 0, P.bcast[0]), (9, 1, P.bcast[1])])

    nl = [P.slot("nl0", "sync"), P.slot("nl1", "sync")]
    ns2 = [P.slot("ns2_0", "gpsimd"), P.slot("ns2_1", "gpsimd")]
    for c in range(DC):
        nl[c % 2].add(16)
        dneed = nl[c % 2].n

        def f_ld(sync, c=c, sl=nl[c % 2], slg=ns2[c % 2], gprev=ns2[c % 2].n):
            if gprev:
                sync.wait_ge(slg.h, gprev)
            sync.dma_start(out=P.ob[c % 2][:], in_=P.h_xn[:, c]).then_inc(sl.h, 16)
        P.emit("sync", f_ld)

        exn = [edge(P, "sv") for _ in range(3)]

        def f_xn(vector, c=c, dneed=dneed, sl=nl[c % 2], exn=exn):
            vector.wait_ge(sl.h, dneed)
            ob = P.ob[c % 2]
            exn[0](vector, vector.tensor_mul(out=ob[:], in0=ob[:],
                                             in1=P.bcast[0][:]))
            exn[1](vector, vector.tensor_sub(out=ob[:], in0=ob[:],
                                             in1=P.bcast[1][:]))
            exn[2](vector, vector.tensor_scalar(out=ob[:], in0=ob[:],
                                                scalar1=P.gb[:, c:c + 1],
                                                scalar2=P.gb[:, DC + c:DC + c + 1],
                                                op0=ALU.mult, op1=ALU.add))
            vector.scalar_tensor_tensor(out=P.xs[:, c], in0=ob[:],
                                        scalar=P.gb[:, 2 * DC + c:2 * DC + c + 1],
                                        in1=P.xs[:, c], op0=ALU.mult,
                                        op1=ALU.add).then_inc(P.sv.h, 1)
        sv.add(1)
        P.emit("vector", f_xn)
        vneed = sv.n

        ns2[c % 2].add(16)

        def f_sp(gpsimd, c=c, vneed=vneed, sl=ns2[c % 2]):
            gpsimd.wait_ge(P.sv.h, vneed)
            gpsimd.dma_start(out=P.h_xn[:, c], in_=P.ob[c % 2][:]).then_inc(sl.h, 16)
        P.emit("gpsimd", f_sp)
    P.barrier()


def ph_attn_proj(P):
    sd, st, sv, ss = P.sd, P.st, P.sv, P.ss
    P.push_scope()
    wstd = P.sbuf("wstd", [128, DC, D], BF16, scoped=True)
    wbc = P.sbuf("wbc", [128, DC, 2 * NST], BF16, scoped=True)

    for c in range(DC):
        def f_ld(sync, c=c):
            sync.dma_start(out=P.ob[0][:, 0:D], in_=P.wdelta[:, c]).then_inc(sd.h, 16)
            sync.dma_start(out=P.ob[1][:, 0:NST], in_=P.wB[:, c]).then_inc(sd.h, 16)
            sync.dma_start(out=P.ob[1][:, NST:2 * NST], in_=P.wC[:, c]).then_inc(sd.h, 16)
        sd.add(48)
        P.emit("sync", f_ld)
        need = sd.n

        def f_cast(vector, c=c, need=need):
            vector.wait_ge(sd.h, need)
            vector.tensor_copy(out=wstd[:, c], in_=P.ob[0][:, 0:D])
            vector.tensor_copy(out=wbc[:, c], in_=P.ob[1][:, 0:2 * NST])
            vector.engine_nop().then_inc(P.sv.h, 1)
        sv.add(1)
        P.emit("vector", f_cast)
        vneed = sv.n

        def f_w(sync, vneed=vneed):
            sync.wait_ge(P.sv.h, vneed)
        P.emit("sync", f_w)
    P.barrier()

    al = [P.slot("al0", "sync"), P.slot("al1", "sync")]
    aw = [P.slot("aw0", "gpsimd"), P.slot("aw1", "gpsimd")]
    ax = [P.slot("ax0", "sync"), P.slot("ax1", "sync")]
    for tt in range(TT4):
        cast_done = []
        for k in range(DC):
            qb = P.cb16[k % 2]
            al[k % 2].add(16)
            dneed = al[k % 2].n

            def f_xl(sync, k=k, tt=tt, sl=al[k % 2],
                     vprev=(cast_done[k - 2] if k >= 2 else 0)):
                if vprev:
                    sync.wait_ge(P.sv.h, vprev)
                sync.dma_start(out=P.ob[k % 2][:, 0:TQ],
                               in_=P.h_xn[:, k, tt * TQ:(tt + 1) * TQ]).then_inc(sl.h, 16)
            P.emit("sync", f_xl)

            def f_cast(vector, k=k, qb=qb, dneed=dneed, sl=al[k % 2],
                       tneed=(st.n if k >= 2 else 0)):
                vector.wait_ge(sl.h, dneed)
                if tneed:
                    vector.wait_ge(P.st.h, tneed - 1)
                vector.tensor_copy(out=qb[:, 0:TQ],
                                   in_=P.ob[k % 2][:, 0:TQ]).then_inc(P.sv.h, 1)
            sv.add(1)
            P.emit("vector", f_cast)
            cast_done.append(sv.n)
            vneed = sv.n

            def f_mm(tensor, k=k, qb=qb, vneed=vneed):
                tensor.wait_ge(P.sv.h, vneed)
                last = None
                for do in range(DC):
                    last = tensor.matmul(P.pmm[do][:],
                                         wstd[:, k, 128 * do:128 * (do + 1)],
                                         qb[:, 0:TQ], start=(k == 0),
                                         stop=(k == DC - 1))
                last.then_inc(P.st.h, 1)
            st.add(1)
            P.emit("tensor", f_mm)
        tneed = st.n

        wr_done = []
        for do in range(DC):
            def f_ev1(scalar, do=do, tneed=tneed, sl=aw[do % 2],
                      gneed=aw[do % 2].n):
                scalar.wait_ge(P.st.h, tneed)
                if gneed:
                    scalar.wait_ge(sl.h, gneed)
                scalar.copy(out=P.ob[do % 2][:, 0:TQ],
                            in_=P.pmm[do][:]).then_inc(P.ss.h, 1)
            ss.add(1)
            P.emit("scalar", f_ev1)
            sneed = ss.n
            ax[do % 2].add(16)
            dneed2 = ax[do % 2].n

            def f_xn2(sync, do=do, tt=tt, sl=ax[do % 2], slw=aw[do % 2],
                      gneed=aw[do % 2].n):
                if gneed:
                    sync.wait_ge(slw.h, gneed)
                sync.dma_start(out=P.cb[do % 2][:, 0:TQ],
                               in_=P.h_xn[:, do, tt * TQ:(tt + 1) * TQ]).then_inc(sl.h, 16)
            P.emit("sync", f_xn2)

            esp = edge(P, "sv")

            def f_sp(vector, do=do, sneed=sneed, slw=aw[(do - 1) % 2], esp=esp,
                     gneed=(aw[(do - 1) % 2].n if do >= 1 else 0)):
                vector.wait_ge(P.ss.h, sneed)
                if gneed:
                    vector.wait_ge(slw.h, gneed)
                z = P.ob[do % 2][:, 0:TQ]
                esp(vector, vector.tensor_scalar(out=z, in0=z,
                                                 scalar1=P.biases[:, do:do + 1],
                                                 scalar2=None, op0=ALU.add))
                vector.tensor_scalar(out=P.cb[2][:, 0:TQ], in0=z, scalar1=0.0,
                                     scalar2=None, op0=ALU.max).then_inc(P.sv.h, 1)
            sv.add(1)
            P.emit("vector", f_sp)
            vneed = sv.n

            es1, es2 = edge(P, "ss"), edge(P, "ss")

            def f_sp2(scalar, do=do, vneed=vneed, es1=es1, es2=es2):
                scalar.wait_ge(P.sv.h, vneed)
                z = P.ob[do % 2][:, 0:TQ]
                es1(scalar, scalar.activation(out=z, in_=z, func=AF.Abs))
                es2(scalar, scalar.activation(out=z, in_=z, func=AF.Exp,
                                              scale=-1.0))
                scalar.activation(out=z, in_=z, func=AF.Ln, bias=1.0).then_inc(P.ss.h, 1)
            ss.add(1)
            P.emit("scalar", f_sp2)
            sneed2 = ss.n

            es3 = edge(P, "sv")

            def f_sp3(vector, do=do, tt=tt, sneed2=sneed2, dneed2=dneed2,
                      sl=ax[do % 2], es3=es3):
                vector.wait_ge(P.ss.h, sneed2)
                vector.wait_ge(sl.h, dneed2)
                z = P.ob[do % 2][:, 0:TQ]
                es3(vector, vector.tensor_add(out=z, in0=z,
                                              in1=P.cb[2][:, 0:TQ]))
                vector.tensor_mul(out=P.cb[2][:, 0:TQ], in0=z,
                                  in1=P.cb[do % 2][:, 0:TQ]).then_inc(P.sv.h, 1)
            sv.add(1)
            P.emit("vector", f_sp3)
            vneed2 = sv.n

            aw[do % 2].add(32)

            def f_wr(gpsimd, do=do, tt=tt, vneed2=vneed2, sl=aw[do % 2]):
                gpsimd.wait_ge(P.sv.h, vneed2)
                gpsimd.dma_start(out=P.h_delta[:, do, tt * TQ:(tt + 1) * TQ],
                                 in_=P.ob[do % 2][:, 0:TQ]).then_inc(sl.h, 16)
                gpsimd.dma_start(out=P.h_dx[:, do, tt * TQ:(tt + 1) * TQ],
                                 in_=P.cb[2][:, 0:TQ]).then_inc(sl.h, 16)
            P.emit("gpsimd", f_wr)
            wr_done.append(0)
        P.barrier()

    for tt in range(TT4):
        cast_done = []
        for k in range(DC):
            al[k % 2].add(16)
            dneed = al[k % 2].n

            def f_xl(sync, k=k, tt=tt, sl=al[k % 2],
                     vprev=(cast_done[k - 2] if k >= 2 else 0)):
                if vprev:
                    sync.wait_ge(P.sv.h, vprev)
                sync.dma_start(out=P.ob[k % 2][:, 0:TQ],
                               in_=P.h_xn[:, k, tt * TQ:(tt + 1) * TQ]).then_inc(sl.h, 16)
            P.emit("sync", f_xl)

            def f_cst(vector, k=k, dneed=dneed, sl=al[k % 2],
                      tneed=(st.n if k >= 2 else 0)):
                vector.wait_ge(sl.h, dneed)
                if tneed:
                    vector.wait_ge(P.st.h, tneed - 1)
                vector.tensor_copy(out=P.cb16[k % 2][:, 0:TQ],
                                   in_=P.ob[k % 2][:, 0:TQ]).then_inc(P.sv.h, 1)
            sv.add(1)
            P.emit("vector", f_cst)
            cast_done.append(sv.n)
            vneed = sv.n

            def f_mmb(tensor, k=k, vneed=vneed):
                tensor.wait_ge(P.sv.h, vneed)
                tensor.matmul(P.pmm[0][0:2 * NST, :], wbc[:, k],
                              P.cb16[k % 2][:, 0:TQ],
                              start=(k == 0),
                              stop=(k == DC - 1)).then_inc(P.st.h, 1)
            st.add(1)
            P.emit("tensor", f_mmb)
        tneed = st.n

        def f_evb(vector, tt=tt, tneed=tneed):
            vector.wait_ge(P.st.h, tneed)
            vector.tensor_scalar(out=P.ob[0][0:2 * NST, 0:TQ],
                                 in0=P.pmm[0][0:2 * NST, :],
                                 scalar1=P.bcb[:, 0:1], scalar2=None,
                                 op0=ALU.add).then_inc(P.sv.h, 1)
        sv.add(1)
        P.emit("vector", f_evb)
        vneed = sv.n

        def f_wbc(sync, tt=tt, vneed=vneed):
            sync.wait_ge(P.sv.h, vneed)
            sync.dma_start(out=P.h_bc[:, tt * TQ:(tt + 1) * TQ],
                           in_=P.ob[0][0:2 * NST, 0:TQ]).then_inc(P.sd.h, 16)
        P.sd.add(16)
        P.emit("sync", f_wbc)
        P.barrier()
    P.pop_scope()


def ph_attn_scan(P):
    sd, sv, ss, sg = P.sd, P.sv, P.ss, P.sg
    P.push_scope()
    nbc = [P.sbuf(f"nbc{i}", [128, T], scoped=True) for i in range(4)]
    ab = P.cb[0]
    ub = P.cb[1]
    hb = P.cb[2]
    db = P.bcast[0]
    xb = P.bcast[1]

    for ng in range(8):
        for j in range(2):
            n = 2 * ng + j
            bcast_row(P, P.h_bc[n:n + 1, :], nbc[2 * j])
            bcast_row(P, P.h_bc[NST + n:NST + n + 1, :], nbc[2 * j + 1])
        P.barrier()
        for c in range(DC):
            def f_ld(sync, c=c):
                sync.dma_start(out=db[:], in_=P.h_delta[:, c]).then_inc(sd.h, 16)
                sync.dma_start(out=xb[:], in_=P.h_dx[:, c]).then_inc(sd.h, 16)
            sd.add(32)
            P.emit("sync", f_ld)
            dneed = sd.n
            for j in range(2):
                n = 2 * ng + j
                col = c * NST + n

                def f_a(scalar, c=c, col=col, dneed=dneed, first=(j == 0)):
                    if first:
                        scalar.wait_ge(sd.h, dneed)
                    scalar.activation(out=ab[:], in_=db[:], func=AF.Exp,
                                      scale=P.acols[:, col:col + 1]).then_inc(P.ss.h, 1)
                ss.add(1)
                P.emit("scalar", f_a)
                sneed = ss.n

                eu1, eu2 = edge(P, "sv"), edge(P, "sv")

                def f_u(vector, j=j, c=c, dneed=dneed, sneed=sneed,
                        eu1=eu1, eu2=eu2):
                    vector.wait_ge(sd.h, dneed)
                    eu1(vector, vector.tensor_mul(out=ub[:], in0=xb[:],
                                                  in1=nbc[2 * j][:]))
                    vector.wait_ge(P.ss.h, sneed)
                    eu2(vector, vector.tensor_tensor_scan(
                        out=hb[:], data0=ab[:], data1=ub[:],
                        initial=0.0, op0=ALU.mult, op1=ALU.add))
                    vector.tensor_mul(out=ab[:], in0=hb[:], in1=nbc[2 * j + 1][:]).then_inc(P.sv.h, 1)
                sv.add(1)
                P.emit("vector", f_u)
                vneed = sv.n

                def f_y(gpsimd, c=c, vneed=vneed):
                    gpsimd.wait_ge(P.sv.h, vneed)
                    gpsimd.tensor_add(out=P.xs[:, c], in0=P.xs[:, c],
                                      in1=ab[:]).then_inc(P.sg.h, 1)
                sg.add(1)
                P.emit("gpsimd", f_y)
                gneed = sg.n

                def f_gw(scalar, gneed=gneed):
                    scalar.wait_ge(P.sg.h, gneed)
                P.emit("scalar", f_gw)
            P.barrier()
    P.barrier()
    P.pop_scope()


def ph_debug_out(P, src):
    sd = P.sd
    sd.add(16)
    need = sd.n

    def f(sync, need=need):
        sync.dma_start(out=P.y_out, in_=src[:]).then_inc(sd.h, 16)
        sync.wait_ge(sd.h, need)
    P.emit("sync", f)
    P.barrier()


def ph_final_wait(P):
    need = P.sd.n

    def f(sync, need=need):
        sync.wait_ge(P.sd.h, need)
    P.emit("sync", f)
    gneed = P.gd.n

    def f2(gpsimd, gneed=gneed):
        gpsimd.wait_ge(P.gd.h, gneed)
    P.emit("gpsimd", f2)
    P.barrier()



def build_program(debug_stage=None):
    P = Prog(debug_stage)
    ph_load(P)
    ph_dequant(P, "g")
    ph_dequant(P, "u")
    ph_dequant(P, "d")
    if debug_stage == "deqr":
        sd = P.sd
        sd.add(16)
        need = sd.n

        def f_t(sync, need=need):
            sync.dma_start(out=P.y_out[0:32, 0, 0:T], in_=P.h_rg[0:32, 0:T]).then_inc(sd.h, 16)
            sync.wait_ge(sd.h, need)
        P.emit("sync", f_t)
        P.barrier()
        ph_final_wait(P)
        return P
    if debug_stage == "deqs":
        sd = P.sd
        sd.add(16)
        need = sd.n

        def f_t(sync, need=need):
            sync.dma_start(out=P.y_out[0:32, 0, 0:T], in_=P.h_sg[0:32, 0:T]).then_inc(sd.h, 16)
            sync.wait_ge(sd.h, need)
        P.emit("sync", f_t)
        P.barrier()
        ph_final_wait(P)
        return P
    if debug_stage == "deq":
        sd, sv = P.sd, P.sv
        for c in range(DC):
            def f_ld(sync, c=c):
                sync.dma_start(out=P.cb16[0][:, 0:T],
                               in_=P.h_wgD[:, c, 0:T]).then_inc(sd.h, 16)
            sd.add(16)
            P.emit("sync", f_ld)
            need = sd.n

            def f_cast(vector, c=c, need=need):
                vector.wait_ge(sd.h, need)
                vector.tensor_copy(out=P.xs[:, c], in_=P.cb16[0][:, 0:T]).then_inc(P.sv.h, 1)
            sv.add(1)
            P.emit("vector", f_cast)
            vneed = sv.n

            def f_w(sync, vneed=vneed):
                sync.wait_ge(P.sv.h, vneed)
            P.emit("sync", f_w)
        P.barrier()
        ph_debug_out(P, P.xs)
        ph_final_wait(P)
        return P
    if debug_stage and debug_stage.startswith("f1_"):
        st = debug_stage[3:]
        off = 0
        if st.endswith("B"):
            off, st = 8, st[:-1]
        elif st.endswith("C"):
            off, st = 16, st[:-1]
        ph_ffn(P, 0, stop=st)
        dbg = P.nc.dram_tensor("dbg_sc", [1, 32], F32, kind="ExternalOutput").ap()
        P.sd.add(16)
        dneed = P.sd.n

        def f_sc(sync, dneed=dneed):
            sync.dma_start(out=dbg[0:1, 0:24], in_=P.sc1[0:1, 0:24]).then_inc(P.sd.h, 16)
            sync.wait_ge(P.sd.h, dneed)
        P.emit("sync", f_sc)
        src = {"gate": P.h_gate, "pchain": P.h_p, "up": P.h_up,
               "opass": P.h_o}.get(st)
        if src is not None:
            P.sd.add(16)
            need = P.sd.n

            def f_h(sync, src=src, off=off, need=need):
                sync.dma_start(out=P.y_out[:, 0:DC, 0:T],
                               in_=src[:, off:off + DC, 0:T]).then_inc(P.sd.h, 16)
                sync.wait_ge(P.sd.h, need)
            P.emit("sync", f_h)
            P.barrier()
        else:
            ph_debug_out(P, P.xs)
        ph_final_wait(P)
        return P
    ph_ffn(P, 0)
    if debug_stage == "x1":
        ph_debug_out(P, P.xs)
        ph_final_wait(P)
        return P
    ph_attn_norms(P)
    ph_attn_proj(P)
    if debug_stage == "attn_pre":
        ph_debug_out(P, P.xs)
        ph_final_wait(P)
        return P
    ph_attn_scan(P)
    if debug_stage == "x2":
        ph_debug_out(P, P.xs)
        ph_final_wait(P)
        return P
    ph_ffn(P, 1)
    ph_final_wait(P)
    return P


def _host_layout(inputs):
    x = np.asarray(inputs["L0_attn_x"])
    wg = np.asarray(inputs["L0_ffn_w_gate"])
    wu = np.asarray(inputs["L0_ffn_w_up"])
    wd = np.asarray(inputs["L0_ffn_w_down"])
    wdel = np.asarray(inputs["L0_attn_W_delta"])
    wB = np.asarray(inputs["L0_attn_W_B"])
    wC = np.asarray(inputs["L0_attn_W_C"])
    alog = np.asarray(inputs["L0_attn_A_log"])
    bdel = np.asarray(inputs["L0_attn_b_delta"])
    bB = np.asarray(inputs["L0_attn_b_B"])
    bC = np.asarray(inputs["L0_attn_b_C"])
    gamma = np.asarray(inputs["L0_attn_gamma"])
    beta = np.asarray(inputs["L0_attn_beta"])
    dskip = np.asarray(inputs["L0_attn_D_skip"])

    def chunks_T(w):
        K, M = w.shape
        return np.ascontiguousarray(w.reshape(K // 128, 128, M).transpose(1, 0, 2))

    shared = {
        "wgT": chunks_T(np.ascontiguousarray(wg.T)),
        "wuT": chunks_T(np.ascontiguousarray(wu.T)),
        "wdT": chunks_T(np.ascontiguousarray(wd.T)),
        "wgN": chunks_T(wg),
        "wuN": chunks_T(wu),
        "wdN": chunks_T(wd),
        "wdelta": chunks_T(wdel),
        "wB": chunks_T(wB),
        "wC": chunks_T(wC),
        "alog": np.ascontiguousarray(
            alog.reshape(DC, 128, NST).transpose(1, 0, 2).reshape(128, DC * NST)),
        "bdelta": np.ascontiguousarray(bdel.reshape(DC, 128).T),
        "bBC": np.concatenate([bB, bC]).reshape(2 * NST, 1),
        "gamma": np.ascontiguousarray(gamma.reshape(DC, 128).T),
        "beta": np.ascontiguousarray(beta.reshape(DC, 128).T),
        "dskip": np.ascontiguousarray(dskip.reshape(DC, 128).T),
        "ident": np.eye(128, dtype=np.float32),
        "fconst": np.tile(np.array([[65536.0, MAGIC, EPS, -0.5]], dtype=np.float32),
                          (128, 1)),
    }
    shared = {k: np.ascontiguousarray(v, dtype=np.float32) for k, v in shared.items()}

    in_maps = []
    for c in range(NCORES):
        xc = np.ascontiguousarray(np.asarray(x[c]).T)
        xc = np.ascontiguousarray(xc.reshape(DC, 128, T).transpose(1, 0, 2))
        m = dict(shared)
        m["x_fm"] = xc
        in_maps.append(m)
    return in_maps


def kernel(**inputs):
    in_maps = _host_layout(inputs)
    P = build_program(DEBUG_STAGE)
    nc = P.finalize()
    res = run_bass_kernel_spmd(nc, in_maps, core_ids=list(range(NCORES)))
    if DEBUG_STAGE:
        _DBG.clear()
        for c in range(NCORES):
            _DBG[c] = {k: np.asarray(v) for k, v in res.results[c].items()}
    outs = []
    for c in range(NCORES):
        y = np.asarray(res.results[c]["y_fm"])
        y = y.transpose(1, 0, 2).reshape(D, T).T
        outs.append(y)
    return np.ascontiguousarray(np.stack(outs, axis=0), dtype=np.float32)


# version 03 (final)
# speedup vs baseline: 1.0602x; 1.0602x over previous
import contextlib
import numpy as np

import concourse.bass as bass
import concourse.mybir as mybir
from concourse.bass_utils import run_bass_kernel_spmd

F32 = mybir.dt.float32
BF16 = mybir.dt.bfloat16
I16 = mybir.dt.int16
AF = mybir.ActivationFunctionType
ALU = mybir.AluOpType
AX = mybir.AxisListType

D = 1024
T = 2048
I = 3072
NST = 16
DC = D // 128
IC = I // 128
TT4 = 4
TQ = T // TT4
MAGIC = float(np.float32(3 * 2 ** 22))
EPS = 1e-5
NCORES = 8

(M_X, S_SX, S_RX, M_G, S_SG, S_RG, S_MSGM, M_U, S_SU, S_RU, M_P, S_SP, S_RPP,
 M_O, S_SO, S_ROP, S_K, S_T0, S_T1, S_TA, S_TB, S_TC, S_ONES_, S_PAD_) = range(24)
NSCAL = 24

DEBUG_STAGE = None
_DBG = {}


class Sem:
    def __init__(self, ctx, nc, name):
        self.h = ctx.enter_context(nc.semaphore(name))
        self.n = 0

    def add(self, k):
        self.n += k
        return self.n


class Prog:
    def __init__(self, debug_stage=None):
        self.debug_stage = debug_stage
        self.nc = bass.Bass("TRN2", target_bir_lowering=False, debug=False)
        self.ctx = contextlib.ExitStack()
        self.scopes = []
        self.ops = {e: [] for e in ("sync", "vector", "scalar", "tensor", "gpsimd")}
        self.phase = "init"
        self.pfx = ""
        self.phase_marks = []
        self.cc_idx = 0
        self._mk_io()
        self._mk_global_sbuf()
        self._mk_sems()

    def _dram(self, name, shape, dtype, kind="Internal"):
        return self.nc.dram_tensor(name, shape, dtype, kind=kind).ap()

    def sbuf(self, name, shape, dt=F32, scoped=False):
        if scoped:
            return self.scopes[-1].enter_context(self.nc.sbuf_tensor(name, shape, dt))
        return self.ctx.enter_context(self.nc.sbuf_tensor(name, shape, dt))

    def push_scope(self):
        self.scopes.append(contextlib.ExitStack())

    def pop_scope(self):
        self.scopes.pop().close()

    def _mk_io(self):
        d = self._dram
        self.x_in = d("x_fm", [128, DC, T], F32, "ExternalInput")
        self.wgT = d("wgT", [128, DC, I], F32, "ExternalInput")
        self.wuT = d("wuT", [128, DC, I], F32, "ExternalInput")
        self.wdT = d("wdT", [128, IC, D], F32, "ExternalInput")
        self.wgN = d("wgN", [128, IC, D], F32, "ExternalInput")
        self.wuN = d("wuN", [128, IC, D], F32, "ExternalInput")
        self.wdN = d("wdN", [128, DC, I], F32, "ExternalInput")
        self.wdelta = d("wdelta", [128, DC, D], F32, "ExternalInput")
        self.wB = d("wB", [128, DC, NST], F32, "ExternalInput")
        self.wC = d("wC", [128, DC, NST], F32, "ExternalInput")
        self.alog = d("alog", [128, DC * NST], F32, "ExternalInput")
        self.bdelta = d("bdelta", [128, DC], F32, "ExternalInput")
        self.bBC = d("bBC", [2 * NST, 1], F32, "ExternalInput")
        self.gamma = d("gamma", [128, DC], F32, "ExternalInput")
        self.beta = d("beta", [128, DC], F32, "ExternalInput")
        self.dskip = d("dskip", [128, DC], F32, "ExternalInput")
        self.ident = d("ident", [128, 128], F32, "ExternalInput")
        self.fconst = d("fconst", [128, 4], F32, "ExternalInput")

        self.y_out = d("y_fm", [128, DC, T], F32, "ExternalOutput")

        self.h_gate = d("h_gate", [128, IC, T], F32)
        self.h_up = d("h_up", [128, IC, T], F32)
        self.h_p = d("h_p", [128, IC, T], F32)
        self.h_o = d("h_o", [128, IC, T], F32)
        self.h_wgD = d("h_wgD", [128, DC, I], BF16)
        self.h_wuD = d("h_wuD", [128, DC, I], BF16)
        self.h_wdD = d("h_wdD", [128, IC, D], BF16)
        self.h_rg = d("h_rg", [32, I], F32)
        self.h_sg = d("h_sg", [32, I], F32)
        self.h_ru = d("h_ru", [32, I], F32)
        self.h_su = d("h_su", [32, I], F32)
        self.h_rd = d("h_rd", [96, D], F32)
        self.h_sd = d("h_sd", [96, D], F32)
        self.h_vec = d("h_vec", [8, T], F32)
        self.h_scal = d("h_scal", [1, NSCAL], F32)
        self.h_col = d("h_col", [128, 1], F32)
        self.h_bc = d("h_bc", [2 * NST, T], F32)
        self.h_delta = d("h_delta", [128, DC, T], F32)
        self.h_xn = d("h_xn", [128, DC, T], F32)
        self.h_dx = d("h_dx", [128, DC, T], F32)
        self.cc_in = [self._dram(f"cc_in{i}", [1, 2], F32) for i in range(8)]
        self.cc_out = [self._dram(f"cc_out{i}", [1, 2], F32) for i in range(8)]

    def _mk_global_sbuf(self):
        self.xs = self.sbuf("xs", [128, DC, T])
        self.cb = [self.sbuf(f"cb{i}", [128, T]) for i in range(3)]
        self.cb16 = [self.sbuf(f"cb16_{i}", [128, T], BF16) for i in range(2)]
        self.ob = [self.sbuf(f"ob{i}", [128, T]) for i in range(2)]
        self.acols = self.sbuf("acols", [128, DC * NST])
        self.biases = self.sbuf("biases", [128, DC])
        self.gb = self.sbuf("gb", [128, 3 * DC])
        self.bcb = self.sbuf("bcb", [2 * NST, 1])
        self.vrt = self.sbuf("vrt", [128, 16 * 10])
        self.vst = self.sbuf("vst", [1, T])
        self.scal = self.sbuf("scal", [128, NSCAL])
        self.sc1 = self.sbuf("sc1", [1, NSCAL])
        self.mx = self.sbuf("mx", [128, 32])
        self.idn = self.sbuf("idn", [128, 128])
        self.fcs = self.sbuf("fcs", [128, 4])
        self.bcast = [self.sbuf(f"bcast{i}", [128, T]) for i in range(2)]
        self.barsc = self.sbuf("barsc", [1, 4])
        ps = lambda name: self.ctx.enter_context(
            self.nc.psum_tensor(name, [128, TQ], F32))
        self.pmm = [ps(f"pmm{i}") for i in range(8)]

    def slot(self, name, eng):
        if name not in self._slots:
            sem = Sem(self.ctx, self.nc, f"sl_{name}")
            sem.eng = eng
            self._slots[name] = sem
        return self._slots[name]

    def _mk_sems(self):
        self._slots = {}
        c, n = self.ctx, self.nc
        self.bar = Sem(c, n, "bar")
        self.sd = Sem(c, n, "sd")
        self.gd = Sem(c, n, "gd")
        self.sv = Sem(c, n, "sv")
        self.ss = Sem(c, n, "ss")
        self.st = Sem(c, n, "st")
        self.sg = Sem(c, n, "sg")

    def emit(self, engine, fn):
        self.ops[engine].append((fn, self.phase))

    def barrier(self):
        sdn, gdn = self.sd.n, self.gd.n

        def _dr_s(sync, sdn=sdn):
            sync.wait_ge(self.sd.h, sdn)

        def _dr_g(gpsimd, gdn=gdn):
            gpsimd.wait_ge(self.gd.h, gdn)
        self.ops["sync"].append((_dr_s, self.phase))
        self.ops["gpsimd"].append((_dr_g, self.phase))
        for sl in self._slots.values():
            if sl.n:
                n = sl.n

                def _dr(eng, sl=sl, n=n):
                    eng.wait_ge(sl.h, n)
                self.ops[sl.eng].append((_dr, self.phase))
        self.phase_marks.append({e: len(self.ops[e]) for e in self.ops})

    def finalize(self):
        nc = self.nc
        marks = self.phase_marks
        nbar = len(marks)

        def bar_inc(eng_name, eng):
            if eng_name in ("vector", "gpsimd"):
                eng.engine_nop().then_inc(self.bar.h, 1)
            elif eng_name == "scalar":
                eng.wait_ge(self.ss.h, self._barsc_init)
                eng.copy(out=self.barsc[0:1, 0:1],
                         in_=self.barsc[0:1, 1:2]).then_inc(self.bar.h, 1)
            else:
                eng.sem_inc(self.bar.h, 1)

        def run(eng_name, eng):
            mi = 0
            for i, (fn, ph) in enumerate(self.ops[eng_name]):
                while mi < nbar and marks[mi][eng_name] == i:
                    bar_inc(eng_name, eng)
                    eng.wait_ge(self.bar.h, 5 * (mi + 1))
                    mi += 1
                with nc.named_scope(ph):
                    fn(eng)
            while mi < nbar:
                bar_inc(eng_name, eng)
                eng.wait_ge(self.bar.h, 5 * (mi + 1))
                mi += 1

        block = self.ctx.enter_context(nc.Block())

        @block.sync
        def _(e):
            run("sync", e)

        @block.vector
        def _(e):
            run("vector", e)

        @block.scalar
        def _(e):
            run("scalar", e)

        @block.tensor
        def _(e):
            run("tensor", e)

        @block.gpsimd
        def _(e):
            run("gpsimd", e)

        while self.scopes:
            self.pop_scope()
        self.ctx.close()
        return nc



def bcast_scalar(P, col):
    sd = P.sd

    def f1(sync, col=col):
        sync.dma_start(out=P.h_scal[0:1, col:col + 1],
                       in_=P.sc1[0:1, col:col + 1]).then_inc(sd.h, 16)
        sd.add(0)
    sd.add(16)
    P.emit("sync", f1)
    need = sd.n

    def f2(sync, col=col, need=need):
        sync.wait_ge(sd.h, need)
        src = P.h_scal[0:1, col:col + 1].broadcast_to([128, 1])
        sync.dma_start(out=P.scal[:, col:col + 1], in_=src).then_inc(sd.h, 16)
    sd.add(16)
    P.emit("sync", f2)


def finalize_mx(P, ncols, sc1_col):
    sd, sv = P.sd, P.sv

    svprev = sv.n

    def f0(vector, svprev=svprev):
        vector.wait_ge(P.sv.h, svprev)
        vector.tensor_reduce(out=P.mx[:, 31:32], in_=P.mx[:, 0:ncols],
                             axis=AX.X, op=ALU.max).then_inc(P.sv.h, 1)
    sv.add(1)
    P.emit("vector", f0)
    vneed = sv.n

    def f1(sync, vneed=vneed):
        sync.wait_ge(P.sv.h, vneed)
        sync.dma_start(out=P.h_col, in_=P.mx[:, 31:32]).then_inc(sd.h, 16)
    sd.add(16)
    P.emit("sync", f1)
    need = sd.n

    def f2(sync, need=need):
        sync.wait_ge(sd.h, need)
        sync.dma_start(out=P.vst[0:1, 0:128],
                       in_=P.h_col.rearrange("p c -> c p")).then_inc(sd.h, 16)
    sd.add(16)
    P.emit("sync", f2)
    need2 = sd.n

    sv.add(1)
    nf3 = sv.n

    def f3(vector, need2=need2, nf3=nf3):
        vector.wait_ge(sd.h, need2)
        vector.tensor_reduce(out=P.sc1[0:1, sc1_col:sc1_col + 1],
                             in_=P.vst[0:1, 0:128], axis=AX.X,
                             op=ALU.max).then_inc(P.sv.h, 1)
        vector.wait_ge(P.sv.h, nf3)
    P.emit("vector", f3)
    P.barrier()


def collective_max(P, sc1_cols):
    P.phase = P.pfx + "coll"
    """AllReduce(max) of sc1[0, cols] across the 8 cores (in place)."""
    idx = P.cc_idx
    P.cc_idx += 1
    nv = len(sc1_cols)
    sd, sg = P.sd, P.sg
    assert nv <= 2
    for j, c in enumerate(sc1_cols):
        def fc(vector, j=j, c=c):
            vector.tensor_copy(out=P.sc1[0:1, S_T0 + j:S_T0 + j + 1],
                               in_=P.sc1[0:1, c:c + 1])
        P.emit("vector", fc)
    P.barrier()

    gd0, sg0 = P.gd.n, P.sg.n

    def f1(gpsimd, idx=idx, nv=nv, gd0=gd0, sg0=sg0):
        gpsimd.dma_start(out=P.cc_in[idx][0:1, 0:2],
                         in_=P.sc1[0:1, S_T0:S_T0 + 2]).then_inc(P.gd.h, 16)
        gpsimd.wait_ge(P.gd.h, gd0 + 16)
        gpsimd.collective_compute(
            "AllReduce", ALU.max, replica_groups=[list(range(NCORES))],
            ins=[P.cc_in[idx]], outs=[P.cc_out[idx]]).then_inc(P.sg.h, 1)
        gpsimd.wait_ge(P.sg.h, sg0 + 1)
        gpsimd.dma_start(out=P.sc1[0:1, S_T0:S_T0 + nv],
                         in_=P.cc_out[idx][0:1, 0:nv]).then_inc(P.gd.h, 16)
        gpsimd.wait_ge(P.gd.h, gd0 + 32)
    P.gd.add(32)
    P.sg.add(1)
    P.emit("gpsimd", f1)
    P.barrier()
    for j, c in enumerate(sc1_cols):
        def fc2(vector, j=j, c=c):
            vector.tensor_copy(out=P.sc1[0:1, c:c + 1],
                               in_=P.sc1[0:1, S_T0 + j:S_T0 + j + 1])
        P.emit("vector", fc2)
    P.barrier()


def qdq_scale_from_max(P, mcol, scol, rcol):
    assert mcol != scol and scol != rcol

    eq = edge(P, "sv")

    def f(vector, eq=eq):
        eq(vector, vector.tensor_scalar(out=P.sc1[0:1, scol:scol + 1],
                                        in0=P.sc1[0:1, mcol:mcol + 1],
                                        scalar1=1.0 / 32767.0, scalar2=1e-12,
                                        op0=ALU.mult, op1=ALU.max))
        vector.reciprocal(out=P.sc1[0:1, rcol:rcol + 1],
                          in_=P.sc1[0:1, scol:scol + 1])
    P.emit("vector", f)
    P.barrier()
    bcast_scalar(P, scol)
    bcast_scalar(P, rcol)
    P.barrier()


def bcast_row(P, hrow_ap, dst):
    if not isinstance(dst, bass.AP):
        dst = dst[:]
    W = dst.shape[-1]

    def f(gpsimd):
        src = hrow_ap.broadcast_to([128, W])
        gpsimd.dma_start(out=dst, in_=src).then_inc(P.gd.h, 16)
    P.gd.add(16)
    P.emit("gpsimd", f)



def edge(P, which):
    sem = getattr(P, which)
    sem.add(1)
    n = sem.n

    def apply(eng, ins, sem=sem, n=n):
        ins.then_inc(sem.h, 1)
        eng.wait_ge(sem.h, n)
    return apply


def vseg(P, k):
    return P.vrt[:, 16 * k:16 * (k + 1)]


def hrow_tiled(P, r):
    return P.h_vec[r:r + 1, :].rearrange("o (p f) -> (o p) f", p=128)


def vseg_to_hrow(P, k, r):
    sd = P.sd

    def f(sync, k=k, r=r):
        sync.dma_start(out=hrow_tiled(P, r), in_=vseg(P, k)).then_inc(sd.h, 16)
    sd.add(16)
    P.emit("sync", f)


def hrow_to_vseg(P, r, k):
    sd = P.sd

    def f(sync, k=k, r=r):
        sync.dma_start(out=vseg(P, k), in_=hrow_tiled(P, r)).then_inc(sd.h, 16)
    sd.add(16)
    P.emit("sync", f)



def ph_load(P):
    P.phase = "load"
    sd = P.sd

    def f_bs(scalar):
        scalar.memzero(P.barsc[:]).then_inc(P.ss.h, 1)
    P.ss.add(1)
    P.emit("scalar", f_bs)
    P._barsc_init = P.ss.n

    def f(sync):
        sync.dma_start(out=P.xs[:], in_=P.x_in).then_inc(sd.h, 16)
        sync.dma_start(out=P.acols[:], in_=P.alog).then_inc(sd.h, 16)
        sync.dma_start(out=P.biases[:], in_=P.bdelta).then_inc(sd.h, 16)
        sync.dma_start(out=P.gb[:, 0:DC], in_=P.gamma).then_inc(sd.h, 16)
        sync.dma_start(out=P.gb[:, DC:2 * DC], in_=P.beta).then_inc(sd.h, 16)
        sync.dma_start(out=P.gb[:, 2 * DC:3 * DC], in_=P.dskip).then_inc(sd.h, 16)
        sync.dma_start(out=P.idn[:], in_=P.ident).then_inc(sd.h, 16)
        sync.dma_start(out=P.fcs[:], in_=P.fconst).then_inc(sd.h, 16)
        sync.dma_start(out=P.bcb[:], in_=P.bBC).then_inc(sd.h, 16)
    sd.add(144)
    P.emit("sync", f)
    P.barrier()

    def f2(scalar):
        scalar.activation(out=P.acols[:], in_=P.acols[:], func=AF.Exp)
    P.emit("scalar", f2)
    P.barrier()

    em = edge(P, "sv")

    def f3(vector, em=em):
        vector.tensor_scalar(out=P.acols[:], in0=P.acols[:], scalar1=-1.0,
                             scalar2=None, op0=ALU.mult)
        em(vector, vector.memset(P.scal[:], 0.0))
        vector.memset(P.sc1[:], 0.0)
        vector.memset(P.vst[:], 0.0)
        vector.memset(P.vrt[:], 0.0)
        vector.memset(P.scal[:, S_ONES_:S_ONES_ + 1], 1.0)
        vector.memset(P.mx[:], 0.0)
    P.emit("vector", f3)
    P.barrier()


def ph_dequant(P, which):
    P.phase = "deq_" + which
    nc = P.nc
    if which == "g":
        wN, wT, hD, hR, hS = P.wgN, P.wgT, P.h_wgD, P.h_rg, P.h_sg
        C, IN, NB, CT, OF = IC, D, D // 32, DC, I
    elif which == "u":
        wN, wT, hD, hR, hS = P.wuN, P.wuT, P.h_wuD, P.h_ru, P.h_su
        C, IN, NB, CT, OF = IC, D, D // 32, DC, I
    else:
        wN, wT, hD, hR, hS = P.wdN, P.wdT, P.h_wdD, P.h_rd, P.h_sd
        C, IN, NB, CT, OF = DC, I, I // 32, IC, D

    sd, sv, ss, st, gd = P.sd, P.sv, P.ss, P.st, P.gd
    P.push_scope()
    big = [P.sbuf(f"dqA_{which}", [128, 3072], scoped=True),
           P.sbuf(f"dqB_{which}", [128, 3072], scoped=True)]
    sb = [P.sbuf(f"dqs0_{which}", [128, 1536], scoped=True),
          P.sbuf(f"dqs1_{which}", [128, 1536], scoped=True)]
    qi = [P.sbuf(f"dqi0_{which}", [128, 1536], I16, scoped=True),
          P.sbuf(f"dqi1_{which}", [128, 1536], I16, scoped=True)]
    d16 = [P.sbuf(f"dq16a_{which}", [128, 1536], BF16, scoped=True),
           P.sbuf(f"dq16b_{which}", [128, 1536], BF16, scoped=True)]
    srow_full = P.sbuf(f"srow_{which}", [96, 3072], scoped=True)
    srow = srow_full[0:NB, 0:C * 128]
    blkmax = P.ob[0]

    GC = 3072 // IN
    NG = C // GC
    sln = [P.slot("dqn0", "sync"), P.slot("dqn1", "sync")]
    vdone1 = []
    for gi in range(NG):
        par = gi % 2
        buf = big[par]
        vprev = vdone1[gi - 2] if gi >= 2 else 0
        sln[par].add(16)
        need = sln[par].n

        def f_load(sync, gi=gi, buf=buf, sl=sln[par], vprev=vprev):
            if vprev:
                sync.wait_ge(P.sv.h, vprev)
            sync.dma_start(
                out=buf[:, 0:3072],
                in_=wN[:, gi * GC:(gi + 1) * GC].rearrange("p a b -> p (a b)")
                ).then_inc(sl.h, 16)
        P.emit("sync", f_load)

        def f_max(vector, buf=buf, need=need, sl=sln[par], gi=gi):
            vector.wait_ge(sl.h, need)
            w = buf[:, 0:3072].rearrange("p (b e) -> p b e", e=32)
            vector.tensor_reduce(out=blkmax[:, gi * 96:(gi + 1) * 96], in_=w,
                                 axis=AX.X, op=ALU.max,
                                 apply_absolute_value=True).then_inc(P.sv.h, 1)
        sv.add(1)
        P.emit("vector", f_max)
        vdone1.append(sv.n)
    P.barrier()

    for c in range(C):
        if c >= 2:
            vneed = sv.n - 1

            def fw2(tensor, vneed=vneed):
                tensor.wait_ge(P.sv.h, vneed)
            P.emit("tensor", fw2)

        def f_tr(tensor, c=c):
            tensor.transpose(out=P.pmm[c % 2][0:NB, 0:128],
                             in_=blkmax[:, c * NB:(c + 1) * NB],
                             identity=P.idn[:]).then_inc(P.st.h, 1)
        st.add(1)
        P.emit("tensor", f_tr)
        tneed = st.n

        def f_ev(vector, c=c, tneed=tneed):
            vector.wait_ge(P.st.h, tneed)
            vector.tensor_scalar(out=srow[0:NB, c * 128:(c + 1) * 128],
                                 in0=P.pmm[c % 2][0:NB, 0:128],
                                 scalar1=1.0 / 7.0, scalar2=1e-12,
                                 op0=ALU.mult, op1=ALU.max).then_inc(P.sv.h, 1)
        sv.add(1)
        P.emit("vector", f_ev)
    P.barrier()

    def f_ws(sync):
        sync.dma_start(out=hS, in_=srow).then_inc(sd.h, 16)
    sd.add(16)
    P.emit("sync", f_ws)
    P.barrier()

    iters = []
    for ct in range(CT):
        for w0 in range(0, OF, 1536):
            iters.append((ct, w0, min(1536, OF - w0)))
    sla = [P.slot("dqa0", "sync"), P.slot("dqa1", "sync")]
    slb = [P.slot("dqb0", "gpsimd"), P.slot("dqb1", "gpsimd")]
    slc = [P.slot("dqc0", "sync"), P.slot("dqc1", "sync")]
    vdone = []
    for i, (ct, w0, W) in enumerate(iters):
        par = i % 2
        da = big[par][:, 0:W]
        rb = big[par][:, 1536:1536 + W]
        sbb = sb[par][:, 0:W]
        qib = qi[par][:, 0:W]
        db = d16[par][:, 0:W]
        b0 = 4 * ct
        vprev = vdone[i - 2] if i >= 2 else 0
        sla[par].add(16)
        aneed = sla[par].n

        def f_ld(sync, ct=ct, w0=w0, W=W, da=da, sl=sla[par], vprev=vprev):
            if vprev:
                sync.wait_ge(P.sv.h, vprev)
            sync.dma_start(out=da, in_=wT[:, ct, w0:w0 + W]).then_inc(sl.h, 16)
        P.emit("sync", f_ld)

        slb[par].add(64)
        bneed = slb[par].n

        def f_bc(gpsimd, b0=b0, w0=w0, W=W, sbb=sbb, sl=slb[par], vprev=vprev):
            if vprev:
                gpsimd.wait_ge(P.sv.h, vprev)
            for j in range(4):
                gpsimd.dma_start(
                    out=sbb[32 * j:32 * (j + 1), 0:W],
                    in_=hS[b0 + j:b0 + j + 1, w0:w0 + W].broadcast_to([32, W])
                    ).then_inc(sl.h, 16)
        P.emit("gpsimd", f_bc)
        cprev = slc[par].n
        nrec = (W + 511) // 512
        es = [edge(P, "sv") for _ in range(nrec + 7)]

        def f_q(vector, da=da, rb=rb, sbb=sbb, qib=qib, db=db, W=W,
                aneed=aneed, bneed=bneed, sa=sla[par], sbl=slb[par],
                cprev=cprev, sc=slc[par], es=es):
            vector.wait_ge(sa.h, aneed)
            vector.wait_ge(sbl.h, bneed)
            if cprev:
                vector.wait_ge(sc.h, cprev)
            k = 0
            for j0 in range(0, W, 512):
                j1 = min(j0 + 512, W)
                es[k](vector, vector.reciprocal(out=rb[:, j0:j1],
                                                in_=sbb[:, j0:j1]))
                k += 1
            es[k](vector, vector.tensor_mul(out=da, in0=da, in1=rb))
            k += 1
            es[k](vector, vector.tensor_scalar(out=rb, in0=da, scalar1=MAGIC,
                                               scalar2=None, op0=ALU.add))
            k += 1
            es[k](vector, vector.tensor_scalar(out=da, in0=rb, scalar1=MAGIC,
                                               scalar2=None, op0=ALU.subtract))
            k += 1
            es[k](vector, vector.tensor_copy(out=qib, in_=da))
            k += 1
            es[k](vector, vector.tensor_scalar(out=da, in0=qib, scalar1=1.0,
                                               scalar2=None, op0=ALU.mult))
            k += 1
            es[k](vector, vector.tensor_mul(out=rb, in0=da, in1=sbb))
            k += 1
            es[k](vector, vector.tensor_copy(out=db, in_=rb))
        P.emit("vector", f_q)
        vdone.append(sv.n)
        sv_after = sv.n
        slc[par].add(16)

        def f_st(sync, ct=ct, w0=w0, W=W, db=db, sv_after=sv_after,
                 sl=slc[par]):
            sync.wait_ge(P.sv.h, sv_after)
            sync.dma_start(out=hD[:, ct, w0:w0 + W], in_=db).then_inc(sl.h, 16)
        P.emit("sync", f_st)
    P.barrier()
    P.pop_scope()


def ph_ln_stats(P, src, sum_seg, sq_seg):
    P.phase = P.pfx + "ln_stats"
    """Column sums over d of src [128, DC, T] (and its square) into vrt
    segments sum_seg/sq_seg ([128,16] tile layout, t = p*16 + f)."""
    st, ss, sv, sd = P.st, P.ss, P.sv, P.sd
    ones = P.scal[:, S_ONES_:S_ONES_ + 1]
    sq = P.ob[1]

    for c in range(DC):
        if c >= 1:
            tprev = st.n

            def fw(scalar, tprev=tprev):
                scalar.wait_ge(P.st.h, tprev)
            P.emit("scalar", fw)

        def f_sq(scalar, c=c):
            scalar.activation(out=sq[:], in_=src[:, c], func=AF.Square).then_inc(P.ss.h, 1)
        ss.add(1)
        P.emit("scalar", f_sq)
        sneed = ss.n

        def f_mm(tensor, c=c, sneed=sneed):
            tensor.wait_ge(P.ss.h, sneed)
            last = None
            for tt in range(TT4):
                tensor.matmul(P.pmm[tt][0:1, :], ones,
                              src[:, c, tt * TQ:(tt + 1) * TQ],
                              start=(c == 0), stop=(c == DC - 1))
                last = tensor.matmul(P.pmm[4 + tt][0:1, :], ones,
                                     sq[:, tt * TQ:(tt + 1) * TQ],
                                     start=(c == 0), stop=(c == DC - 1))
            last.then_inc(P.st.h, 1)
        st.add(1)
        P.emit("tensor", f_mm)
    tneed = st.n
    ph_stats_evac(P, tneed, sum_seg, sq_seg)


def ph_stats_evac(P, tneed, sum_seg, sq_seg):
    sv, sd = P.sv, P.sd

    def f_ev(vector, tneed=tneed):
        vector.wait_ge(P.st.h, tneed)
        for tt in range(TT4):
            vector.tensor_copy(out=P.vst[0:1, tt * TQ:(tt + 1) * TQ],
                               in_=P.pmm[tt][0:1, :])
        vector.engine_nop().then_inc(P.sv.h, 1)
    sv.add(1)
    P.emit("vector", f_ev)
    vneed = sv.n

    def f_w(sync, vneed=vneed):
        sync.wait_ge(P.sv.h, vneed)
        sync.dma_start(out=P.h_vec[6:7, :], in_=P.vst[0:1, :]).then_inc(sd.h, 16)
    sd.add(16)
    P.emit("sync", f_w)
    P.barrier()

    def f_ev2(vector):
        for tt in range(TT4):
            vector.tensor_copy(out=P.vst[0:1, tt * TQ:(tt + 1) * TQ],
                               in_=P.pmm[4 + tt][0:1, :])
        vector.engine_nop().then_inc(P.sv.h, 1)
    sv.add(1)
    P.emit("vector", f_ev2)
    vneed2 = sv.n

    def f_w2(sync, vneed2=vneed2):
        sync.wait_ge(P.sv.h, vneed2)
        sync.dma_start(out=P.h_vec[7:8, :], in_=P.vst[0:1, :]).then_inc(sd.h, 16)
    sd.add(16)
    P.emit("sync", f_w2)
    P.barrier()
    hrow_to_vseg(P, 6, sum_seg)
    hrow_to_vseg(P, 7, sq_seg)
    P.barrier()


def ph_mu_rho(P, sum_seg, sq_seg, mu_seg, rho_seg):
    EX2, MSQ, VAR = 4, 5, 6

    ev = [edge(P, "sv") for _ in range(3)]

    def fv(vector, ev=ev):
        ev[0](vector, vector.tensor_scalar(out=vseg(P, mu_seg),
                                           in0=vseg(P, sum_seg),
                                           scalar1=1.0 / D, scalar2=None,
                                           op0=ALU.mult))
        ev[1](vector, vector.tensor_scalar(out=vseg(P, EX2), in0=vseg(P, sq_seg),
                                           scalar1=1.0 / D, scalar2=None,
                                           op0=ALU.mult))
        ev[2](vector, vector.tensor_mul(out=vseg(P, MSQ), in0=vseg(P, mu_seg),
                                        in1=vseg(P, mu_seg)))
        vector.tensor_sub(out=vseg(P, VAR), in0=vseg(P, EX2),
                          in1=vseg(P, MSQ)).then_inc(P.sv.h, 1)
    P.sv.add(1)
    P.emit("vector", fv)
    vneed = P.sv.n
    e2 = edge(P, "ss")

    def fs(scalar, vneed=vneed, e2=e2):
        scalar.wait_ge(P.sv.h, vneed)
        e2(scalar, scalar.activation(out=vseg(P, EX2), in_=vseg(P, VAR),
                                     func=AF.Ln, bias=P.fcs[:, 2:3]))
        scalar.activation(out=vseg(P, rho_seg), in_=vseg(P, EX2), func=AF.Exp,
                          scale=P.fcs[:, 3:4]).then_inc(P.ss.h, 1)
    P.ss.add(1)
    P.emit("scalar", fs)
    P.barrier()


def ph_vec_to_hbm_and_bcast(P, segs_rows_dsts):
    for seg, r, dst in segs_rows_dsts:
        vseg_to_hrow(P, seg, r)
    P.barrier()
    for seg, r, dst in segs_rows_dsts:
        bcast_row(P, P.h_vec[r:r + 1, :], dst)
    P.barrier()



def ph_ffn_xq(P, src, xq):
    P.phase = P.pfx + "ffn_xq"
    """qdq(LN(src)) -> xq bf16. Needs mu/rho segments (2,3); PR/QR = 7/8."""
    MU, RHO, PR, QR = 2, 3, 7, 8
    def f_pq(vector):
        vector.tensor_mul(out=vseg(P, QR), in0=vseg(P, MU), in1=vseg(P, RHO))
        vector.tensor_copy(out=vseg(P, PR), in_=vseg(P, RHO))
    P.emit("vector", f_pq)
    P.barrier()
    ph_vec_to_hbm_and_bcast(P, [(PR, 0, P.bcast[0]), (QR, 1, P.bcast[1])])

    edges_mx = [(edge(P, "sv"), edge(P, "sv"), edge(P, "sv")) for _ in range(DC)]

    def f_mx(vector, edges=edges_mx):
        for c in range(DC):
            ea, eb, ec = edges[c]
            ea(vector, vector.tensor_mul(out=P.ob[0][:], in0=src[:, c],
                                         in1=P.bcast[0][:]))
            eb(vector, vector.tensor_sub(out=P.ob[0][:], in0=P.ob[0][:],
                                         in1=P.bcast[1][:]))
            ec(vector, vector.tensor_reduce(out=P.mx[:, c:c + 1], in_=P.ob[0][:],
                                            axis=AX.X, op=ALU.max,
                                            apply_absolute_value=True))
    P.emit("vector", f_mx)
    P.barrier()
    finalize_mx(P, DC, M_X)
    collective_max(P, [M_X])
    qdq_scale_from_max(P, M_X, S_SX, S_RX)

    ep = edge(P, "sv")

    def f_prow(vector, ep=ep):
        ep(vector, vector.tensor_scalar(out=vseg(P, PR), in0=vseg(P, RHO),
                                        scalar1=P.scal[:, S_RX:S_RX + 1],
                                        scalar2=None, op0=ALU.mult))
        vector.tensor_mul(out=vseg(P, QR), in0=vseg(P, MU), in1=vseg(P, PR))
    P.emit("vector", f_prow)
    P.barrier()
    ph_vec_to_hbm_and_bcast(P, [(PR, 0, P.bcast[0]), (QR, 1, P.bcast[1])])

    edges_q = [(edge(P, "sv"), edge(P, "sv"), edge(P, "sv"), edge(P, "sv"))
               for _ in range(DC)]

    def f_q(vector, edges=edges_q):
        for c in range(DC):
            ea, eb, ec, ed = edges[c]
            ea(vector, vector.tensor_mul(out=P.ob[0][:], in0=src[:, c],
                                         in1=P.bcast[0][:]))
            eb(vector, vector.tensor_sub(out=P.ob[0][:], in0=P.ob[0][:],
                                         in1=P.bcast[1][:]))
            ec(vector, vector.tensor_scalar(out=P.ob[0][:], in0=P.ob[0][:],
                                            scalar1=MAGIC, scalar2=MAGIC,
                                            op0=ALU.add, op1=ALU.subtract))
            ed(vector, vector.tensor_scalar(out=xq[:, c], in0=P.ob[0][:],
                                            scalar1=P.scal[:, S_SX:S_SX + 1],
                                            scalar2=None, op0=ALU.mult))
    P.emit("vector", f_q)
    P.barrier()


def ph_ffn_mm_up(P, xq, wD_hbm, spill, mcol, wst):
    P.phase = P.pfx + "ffn_mm_" + ("g" if mcol == M_G else "u")
    """gate/up matmul: out[i, t] = sum_d Wdeq[d, i] xq[d, t]; spill + absmax
    of the result into sc1[mcol]. wst holds HALF the weights [128, 8, 1536]."""
    sd, st, ss, sv = P.sd, P.st, P.ss, P.sv

    evac_done = []
    red_done = []
    spill_done = []
    slw = P.slot(f"fwl_{mcol}", "sync")
    sp0 = P.slot(f"fsp0_{mcol}", "gpsimd")
    sp1 = P.slot(f"fsp1_{mcol}", "gpsimd")
    sps = [sp0, sp1]
    for half in range(2):
        i0 = half * (IC // 2)
        slw.add(16)
        sdn = slw.n
        if half == 1:
            ssprev = ss.n

            def fw_h(sync, ssprev=ssprev):
                sync.wait_ge(P.ss.h, ssprev)
            P.emit("sync", fw_h)

        def f_ldw(sync, half=half, sdn=sdn, slw=slw):
            sync.dma_start(out=wst[:],
                           in_=wD_hbm[:, :, half * 1536:(half + 1) * 1536]
                           ).then_inc(slw.h, 16)
            sync.wait_ge(slw.h, sdn)
        P.emit("sync", f_ldw)
        if half == 0:
            P.barrier()
        else:
            def fw_t(tensor, sdn=sdn, slw=slw):
                tensor.wait_ge(slw.h, sdn)
            P.emit("tensor", fw_t)

        for ii in range(IC // 2):
            i = i0 + ii
            grp = 4 * (i % 2)
            if i >= 2:
                need = evac_done[i - 2]

                def fwp(tensor, need=need):
                    tensor.wait_ge(P.ss.h, need)
                P.emit("tensor", fwp)

            def f_mm(tensor, ii=ii, grp=grp):
                last = None
                for k in range(DC):
                    stat = wst[:, k, 128 * ii:128 * (ii + 1)]
                    for tt in range(TT4):
                        last = tensor.matmul(P.pmm[grp + tt][:],
                                             stat, xq[:, k, tt * TQ:(tt + 1) * TQ],
                                             start=(k == 0), stop=(k == DC - 1))
                last.then_inc(P.st.h, 1)
            st.add(1)
            P.emit("tensor", f_mm)
            tneed = st.n
            buf = P.cb[i % 2]

            def f_ev(scalar, i=i, grp=grp, tneed=tneed, buf=buf,
                     sl=sps[i % 2], gneed=sps[i % 2].n,
                     vneed=(red_done[i - 2] if i >= 2 else 0)):
                scalar.wait_ge(P.st.h, tneed)
                if gneed:
                    scalar.wait_ge(sl.h, gneed)
                if vneed:
                    scalar.wait_ge(P.sv.h, vneed)
                last = None
                for tt in range(TT4):
                    last = scalar.copy(out=buf[:, tt * TQ:(tt + 1) * TQ],
                                       in_=P.pmm[grp + tt][:])
                last.then_inc(P.ss.h, 1)
            ss.add(1)
            P.emit("scalar", f_ev)
            evac_done.append(ss.n)
            sneed = ss.n

            def f_red(vector, i=i, buf=buf, sneed=sneed):
                vector.wait_ge(P.ss.h, sneed)
                vector.tensor_reduce(out=P.mx[:, i:i + 1], in_=buf[:], axis=AX.X,
                                     op=ALU.max,
                                     apply_absolute_value=True).then_inc(P.sv.h, 1)
            sv.add(1)
            P.emit("vector", f_red)
            red_done.append(sv.n)

            sps[i % 2].add(16)

            def f_sp(gpsimd, i=i, buf=buf, sneed=sneed, sl=sps[i % 2]):
                gpsimd.wait_ge(P.ss.h, sneed)
                gpsimd.dma_start(out=spill[:, i], in_=buf[:]).then_inc(sl.h, 16)
            P.emit("gpsimd", f_sp)
            spill_done.append(0)
    P.barrier()
    finalize_mx(P, IC, mcol)


def ph_pchain(P):
    P.phase = P.pfx + "ffn_pchain"
    """gate -> p_scaled (spilled); needs s_g/r_g/msgM scalars ready."""
    sd, sv, ss = P.sd, P.sv, P.ss
    def f_m(vector):
        vector.tensor_scalar(out=P.sc1[0:1, S_MSGM:S_MSGM + 1],
                             in0=P.sc1[0:1, S_SG:S_SG + 1],
                             scalar1=-MAGIC, scalar2=None, op0=ALU.mult)
    P.emit("vector", f_m)
    P.barrier()
    bcast_scalar(P, S_MSGM)
    P.barrier()

    pl = [P.slot("pl0", "sync"), P.slot("pl1", "sync")]
    psp = [P.slot("psp0", "gpsimd"), P.slot("psp1", "gpsimd")]
    for i in range(IC):
        buf = P.cb[i % 2]
        sig = P.cb[2]
        gneed = psp[i % 2].n

        def fw(sync, gneed=gneed, sl=psp[i % 2]):
            if gneed:
                sync.wait_ge(sl.h, gneed)
        P.emit("sync", fw)
        pl[i % 2].add(16)
        dneed = pl[i % 2].n

        def f_ld(sync, i=i, buf=buf, sl=pl[i % 2]):
            sync.dma_start(out=buf[:], in_=P.h_gate[:, i]).then_inc(sl.h, 16)
        P.emit("sync", f_ld)

        def f_w1(vector, buf=buf, dneed=dneed, sl=pl[i % 2]):
            vector.wait_ge(sl.h, dneed)
            vector.tensor_scalar(out=buf[:], in0=buf[:],
                                 scalar1=P.scal[:, S_RG:S_RG + 1], scalar2=MAGIC,
                                 op0=ALU.mult, op1=ALU.add).then_inc(P.sv.h, 1)
        sv.add(1)
        P.emit("vector", f_w1)
        vneed = sv.n

        es = edge(P, "ss")

        def f_sig(scalar, buf=buf, sig=sig, vneed=vneed, es=es):
            scalar.wait_ge(P.sv.h, vneed)
            es(scalar, scalar.activation(out=sig[:], in_=buf[:], func=AF.Sigmoid,
                                         scale=P.scal[:, S_SG:S_SG + 1],
                                         bias=P.scal[:, S_MSGM:S_MSGM + 1]))
            scalar.activation(out=sig[:], in_=sig[:], func=AF.Identity,
                              scale=P.fcs[:, 0:1],
                              bias=P.fcs[:, 1:2]).then_inc(P.ss.h, 1)
        ss.add(1)
        P.emit("scalar", f_sig)
        sneed = ss.n

        ep0, ep1, ep2 = edge(P, "sv"), edge(P, "sv"), edge(P, "sv")

        def f_p(vector, i=i, buf=buf, sig=sig, sneed=sneed,
                ep0=ep0, ep1=ep1, ep2=ep2):
            vector.wait_ge(P.ss.h, sneed)
            ep0(vector, vector.tensor_scalar(out=buf[:], in0=buf[:], scalar1=MAGIC,
                                             scalar2=None, op0=ALU.subtract))
            ep1(vector, vector.tensor_scalar(out=sig[:], in0=sig[:], scalar1=MAGIC,
                                             scalar2=65535.0, op0=ALU.subtract,
                                             op1=ALU.min))
            ep2(vector, vector.tensor_mul(out=buf[:], in0=buf[:], in1=sig[:]))
            vector.tensor_reduce(out=P.mx[:, i:i + 1], in_=buf[:], axis=AX.X,
                                 op=ALU.max,
                                 apply_absolute_value=True).then_inc(P.sv.h, 1)
        sv.add(1)
        P.emit("vector", f_p)
        vneed2 = sv.n

        psp[i % 2].add(16)

        def f_sp(gpsimd, i=i, buf=buf, vneed2=vneed2, sl=psp[i % 2]):
            gpsimd.wait_ge(P.sv.h, vneed2)
            gpsimd.dma_start(out=P.h_p[:, i], in_=buf[:]).then_inc(sl.h, 16)
        P.emit("gpsimd", f_sp)
    P.barrier()
    finalize_mx(P, IC, M_P)


def ph_opass(P):
    P.phase = P.pfx + "ffn_opass"
    """p_scaled & up -> o_scaled (spilled) + absmax. Needs r_u, rpp scalars."""
    sd, sv = P.sd, P.sv
    ol = [P.slot("ol0", "sync"), P.slot("ol1", "sync")]
    osp = [P.slot("osp0", "gpsimd"), P.slot("osp1", "gpsimd")]
    for i in range(IC):
        pb = P.cb[i % 2]
        ub = P.ob[i % 2]
        gneed = osp[i % 2].n

        def fw(sync, gneed=gneed, sl=osp[i % 2]):
            if gneed:
                sync.wait_ge(sl.h, gneed)
        P.emit("sync", fw)
        ol[i % 2].add(32)
        dneed = ol[i % 2].n

        def f_ld(sync, i=i, pb=pb, ub=ub, sl=ol[i % 2]):
            sync.dma_start(out=pb[:], in_=P.h_p[:, i]).then_inc(sl.h, 16)
            sync.dma_start(out=ub[:], in_=P.h_up[:, i]).then_inc(sl.h, 16)
        P.emit("sync", f_ld)

        eo = [edge(P, "sv") for _ in range(5)]

        def f_o(vector, i=i, pb=pb, ub=ub, dneed=dneed, sl=ol[i % 2], eo=eo):
            vector.wait_ge(sl.h, dneed)
            eo[0](vector, vector.tensor_scalar(out=pb[:], in0=pb[:],
                                               scalar1=P.scal[:, S_RPP:S_RPP + 1],
                                               scalar2=MAGIC,
                                               op0=ALU.mult, op1=ALU.add))
            eo[1](vector, vector.tensor_scalar(out=ub[:], in0=ub[:],
                                               scalar1=P.scal[:, S_RU:S_RU + 1],
                                               scalar2=MAGIC, op0=ALU.mult,
                                               op1=ALU.add))
            eo[2](vector, vector.tensor_scalar(out=pb[:], in0=pb[:], scalar1=MAGIC,
                                               scalar2=None, op0=ALU.subtract))
            eo[3](vector, vector.tensor_scalar(out=ub[:], in0=ub[:], scalar1=MAGIC,
                                               scalar2=None, op0=ALU.subtract))
            eo[4](vector, vector.tensor_mul(out=pb[:], in0=pb[:], in1=ub[:]))
            vector.tensor_reduce(out=P.mx[:, i:i + 1], in_=pb[:], axis=AX.X,
                                 op=ALU.max,
                                 apply_absolute_value=True).then_inc(P.sv.h, 1)
        sv.add(1)
        P.emit("vector", f_o)
        vneed2 = sv.n

        osp[i % 2].add(16)

        def f_sp(gpsimd, i=i, pb=pb, vneed2=vneed2, sl=osp[i % 2]):
            gpsimd.wait_ge(P.sv.h, vneed2)
            gpsimd.dma_start(out=P.h_o[:, i], in_=pb[:]).then_inc(sl.h, 16)
        P.emit("gpsimd", f_sp)
    P.barrier()
    finalize_mx(P, IC, M_O)


def ph_down(P, wst, dst_xs, write_out):
    P.phase = P.pfx + "ffn_down"
    """o_scaled -> o_q bf16 -> down matmul; residual x + 0.5*down.
    wst holds HALF the k-chunks [128, 12, 1024]."""
    sd, st, sv = P.sd, P.st, P.sv

    dwl = P.slot("dwl", "sync")
    dol = [P.slot("dol0", "sync"), P.slot("dol1", "sync")]
    for tt in range(TT4):
        conv_done = []
        mm_done = []
        for half in range(2):
            k0 = half * 12
            dwl.add(16)
            sdn = dwl.n
            tprev = st.n

            def f_ldw(sync, half=half, sdn=sdn, tprev=tprev, dwl=dwl):
                if tprev:
                    sync.wait_ge(P.st.h, tprev)
                sync.dma_start(out=wst[:],
                               in_=P.h_wdD[:, 12 * half:12 * (half + 1), :]
                               ).then_inc(dwl.h, 16)
            P.emit("sync", f_ldw)

            def fw_t(tensor, sdn=sdn, dwl=dwl):
                tensor.wait_ge(dwl.h, sdn)
            P.emit("tensor", fw_t)

            for kk in range(12):
                k = k0 + kk
                ob = P.ob[k % 2]
                qb = P.cb16[k % 2]
                if k >= 2:
                    vneed = conv_done[k - 2]

                    def fw(sync, vneed=vneed):
                        sync.wait_ge(P.sv.h, vneed)
                    P.emit("sync", fw)
                dol[k % 2].add(16)
                dneed = dol[k % 2].n

                def f_ld(sync, k=k, tt=tt, ob=ob, sl=dol[k % 2]):
                    sync.dma_start(out=ob[:, 0:TQ],
                                   in_=P.h_o[:, k, tt * TQ:(tt + 1) * TQ]).then_inc(sl.h, 16)
                P.emit("sync", f_ld)

                ed = edge(P, "sv")

                def f_q(vector, ob=ob, qb=qb, dneed=dneed, sl=dol[k % 2], ed=ed,
                        tneed=(mm_done[k - 2] if k >= 2 else 0)):
                    vector.wait_ge(sl.h, dneed)
                    if tneed:
                        vector.wait_ge(P.st.h, tneed)
                    ed(vector, vector.tensor_scalar(
                        out=ob[:, 0:TQ], in0=ob[:, 0:TQ],
                        scalar1=P.scal[:, S_ROP:S_ROP + 1],
                        scalar2=MAGIC, op0=ALU.mult, op1=ALU.add))
                    vector.tensor_scalar(out=qb[:, 0:TQ], in0=ob[:, 0:TQ],
                                         scalar1=MAGIC,
                                         scalar2=P.scal[:, S_SO:S_SO + 1],
                                         op0=ALU.subtract, op1=ALU.mult).then_inc(P.sv.h, 1)
                sv.add(1)
                P.emit("vector", f_q)
                conv_done.append(sv.n)
                vneed2 = sv.n

                def f_mm(tensor, kk=kk, k=k, qb=qb, vneed2=vneed2):
                    tensor.wait_ge(P.sv.h, vneed2)
                    last = None
                    for do in range(DC):
                        last = tensor.matmul(P.pmm[do][:],
                                             wst[:, kk, 128 * do:128 * (do + 1)],
                                             qb[:, 0:TQ], start=(k == 0),
                                             stop=(k == IC - 1))
                    last.then_inc(P.st.h, 1)
                st.add(1)
                P.emit("tensor", f_mm)
                mm_done.append(st.n)
        tneed = st.n

        def f_ev(vector, tt=tt, tneed=tneed):
            vector.wait_ge(P.st.h, tneed)
            for do in range(DC):
                dst = dst_xs[:, do, tt * TQ:(tt + 1) * TQ]
                vector.scalar_tensor_tensor(out=dst, in0=P.pmm[do][:], scalar=0.5,
                                            in1=P.xs[:, do, tt * TQ:(tt + 1) * TQ],
                                            op0=ALU.mult, op1=ALU.add)
            vector.engine_nop().then_inc(P.sv.h, 1)
        sv.add(1)
        P.emit("vector", f_ev)
        if write_out:
            vneed3 = sv.n

            def f_wo(sync, tt=tt, vneed3=vneed3):
                sync.wait_ge(P.sv.h, vneed3)
                for do in range(DC):
                    sync.dma_start(out=P.y_out[:, do, tt * TQ:(tt + 1) * TQ],
                                   in_=dst_xs[:, do, tt * TQ:(tt + 1) * TQ]
                                   ).then_inc(sd.h, 16)
            sd.add(16 * DC)
            P.emit("sync", f_wo)
        P.barrier()
    P.barrier()


def ph_ffn(P, call, stop=None):
    P.push_scope()
    P.pfx = "f%d_" % call
    xq = P.sbuf(f"xq{call}", [128, DC, T], BF16, scoped=True)
    wst = P.sbuf(f"wst{call}", [128, 12288], BF16, scoped=True)
    wst_gu = wst[:].rearrange("p (a b) -> p a b", a=DC)
    wst_d = wst[:].rearrange("p (a b) -> p a b", a=12)

    ph_ln_stats(P, P.xs, 0, 1)
    ph_mu_rho(P, 0, 1, 2, 3)
    if stop == "stats":
        P.pop_scope()
        return
    ph_ffn_xq(P, P.xs, xq)
    if stop == "xq":
        P.pop_scope()
        return
    ph_ffn_mm_up(P, xq, P.h_wgD, P.h_gate, M_G, wst_gu)
    if stop == "gate":
        P.pop_scope()
        return
    collective_max(P, [M_G])
    qdq_scale_from_max(P, M_G, S_SG, S_RG)
    if stop == "coll":
        P.pop_scope()
        return
    ph_pchain(P)
    if stop == "pchain":
        P.pop_scope()
        return
    ph_ffn_mm_up(P, xq, P.h_wuD, P.h_up, M_U, wst_gu)
    if stop == "up":
        P.pop_scope()
        return
    collective_max(P, [M_U, M_P])
    qdq_scale_from_max(P, M_U, S_SU, S_RU)

    efs = [edge(P, "sv") for _ in range(4)]

    def f_sp(vector, efs=efs):
        efs[0](vector, vector.tensor_scalar(out=P.sc1[0:1, S_K:S_K + 1],
                                            in0=P.sc1[0:1, S_SG:S_SG + 1],
                                            scalar1=1.0 / 65536.0, scalar2=None,
                                            op0=ALU.mult))
        efs[1](vector, vector.tensor_mul(out=P.sc1[0:1, S_TA:S_TA + 1],
                                         in0=P.sc1[0:1, M_P:M_P + 1],
                                         in1=P.sc1[0:1, S_K:S_K + 1]))
        efs[2](vector, vector.tensor_scalar(out=P.sc1[0:1, S_SP:S_SP + 1],
                                            in0=P.sc1[0:1, S_TA:S_TA + 1],
                                            scalar1=1.0 / 32767.0, scalar2=1e-12,
                                            op0=ALU.mult, op1=ALU.max))
        efs[3](vector, vector.reciprocal(out=P.sc1[0:1, S_TB:S_TB + 1],
                                         in_=P.sc1[0:1, S_SP:S_SP + 1]))
        vector.tensor_mul(out=P.sc1[0:1, S_RPP:S_RPP + 1],
                          in0=P.sc1[0:1, S_TB:S_TB + 1],
                          in1=P.sc1[0:1, S_K:S_K + 1])
    P.emit("vector", f_sp)
    P.barrier()
    bcast_scalar(P, S_RPP)
    bcast_scalar(P, S_RU)
    P.barrier()
    ph_opass(P)
    if stop == "opass":
        P.pop_scope()
        return

    eso = edge(P, "sv")

    def f_so(vector, eso=eso):
        eso(vector, vector.tensor_mul(out=P.sc1[0:1, S_TA:S_TA + 1],
                                      in0=P.sc1[0:1, M_O:M_O + 1],
                                      in1=P.sc1[0:1, S_SP:S_SP + 1]))
        vector.tensor_mul(out=P.sc1[0:1, M_O:M_O + 1],
                          in0=P.sc1[0:1, S_TA:S_TA + 1],
                          in1=P.sc1[0:1, S_SU:S_SU + 1])
    P.emit("vector", f_so)
    P.barrier()
    collective_max(P, [M_O])
    qdq_scale_from_max(P, M_O, S_SO, S_TC)
    erop = edge(P, "sv")

    def f_rop(vector, erop=erop):
        erop(vector, vector.tensor_mul(out=P.sc1[0:1, S_TA:S_TA + 1],
                                       in0=P.sc1[0:1, S_TC:S_TC + 1],
                                       in1=P.sc1[0:1, S_SP:S_SP + 1]))
        vector.tensor_mul(out=P.sc1[0:1, S_ROP:S_ROP + 1],
                          in0=P.sc1[0:1, S_TA:S_TA + 1],
                          in1=P.sc1[0:1, S_SU:S_SU + 1])
    P.emit("vector", f_rop)
    P.barrier()
    bcast_scalar(P, S_ROP)
    bcast_scalar(P, S_SO)
    P.barrier()
    ph_down(P, wst_d, P.xs, write_out=(call == 1))
    P.pop_scope()



def ph_attn_norms(P):
    P.pfx = "a_"
    P.phase = "attn_norms"
    """xln = LN(x1); xs += xln; xln stats; xn = gamma*LN(xln)+beta;
    xs += xn*dskip. xln/xn stream through h_xn (no big resident buffer)."""
    sd, st, ss, sv = P.sd, P.st, P.ss, P.sv
    ones = P.scal[:, S_ONES_:S_ONES_ + 1]

    ph_ln_stats(P, P.xs, 0, 1)
    ph_mu_rho(P, 0, 1, 2, 3)

    def f_c1(vector):
        vector.tensor_mul(out=vseg(P, 9), in0=vseg(P, 2), in1=vseg(P, 3))
    P.emit("vector", f_c1)
    P.barrier()
    ph_vec_to_hbm_and_bcast(P, [(3, 0, P.bcast[0]), (9, 1, P.bcast[1])])

    ns = [P.slot("ns0", "gpsimd"), P.slot("ns1", "gpsimd")]
    mmst = []
    for c in range(DC):
        exl = [edge(P, "sv") for _ in range(2)]

        def f_xln(vector, c=c, sl=ns[c % 2], gprev=ns[c % 2].n, exl=exl,
                  tprev=(mmst[c - 2] if c >= 2 else 0)):
            if gprev:
                vector.wait_ge(sl.h, gprev)
            if tprev:
                vector.wait_ge(P.st.h, tprev)
            ob = P.ob[c % 2]
            exl[0](vector, vector.tensor_mul(out=ob[:], in0=P.xs[:, c],
                                             in1=P.bcast[0][:]))
            exl[1](vector, vector.tensor_sub(out=ob[:], in0=ob[:],
                                             in1=P.bcast[1][:]))
            vector.tensor_add(out=P.xs[:, c], in0=P.xs[:, c],
                              in1=ob[:]).then_inc(P.sv.h, 1)
        sv.add(1)
        P.emit("vector", f_xln)
        vneed = sv.n

        def f_sq(scalar, c=c, vneed=vneed, tprev=(st.n if c >= 1 else 0)):
            scalar.wait_ge(P.sv.h, vneed)
            if tprev:
                scalar.wait_ge(P.st.h, tprev)
            scalar.activation(out=P.cb[2][:], in_=P.ob[c % 2][:],
                              func=AF.Square).then_inc(P.ss.h, 1)
        ss.add(1)
        P.emit("scalar", f_sq)
        sneed = ss.n

        def f_mm(tensor, c=c, sneed=sneed):
            tensor.wait_ge(P.ss.h, sneed)
            last = None
            for tt in range(TT4):
                tensor.matmul(P.pmm[tt][0:1, :], ones,
                              P.ob[c % 2][:, tt * TQ:(tt + 1) * TQ],
                              start=(c == 0), stop=(c == DC - 1))
                last = tensor.matmul(P.pmm[4 + tt][0:1, :], ones,
                                     P.cb[2][:, tt * TQ:(tt + 1) * TQ],
                                     start=(c == 0), stop=(c == DC - 1))
            last.then_inc(P.st.h, 1)
        st.add(1)
        P.emit("tensor", f_mm)
        mmst.append(st.n)

        ns[c % 2].add(16)

        def f_sp(gpsimd, c=c, sneed=sneed, sl=ns[c % 2]):
            gpsimd.wait_ge(P.ss.h, sneed)
            gpsimd.dma_start(out=P.h_xn[:, c], in_=P.ob[c % 2][:]).then_inc(sl.h, 16)
        P.emit("gpsimd", f_sp)
    tneed = st.n
    ph_stats_evac(P, tneed, 0, 1)
    ph_mu_rho(P, 0, 1, 2, 3)

    def f_c2(vector):
        vector.tensor_mul(out=vseg(P, 9), in0=vseg(P, 2), in1=vseg(P, 3))
    P.emit("vector", f_c2)
    P.barrier()
    ph_vec_to_hbm_and_bcast(P, [(3, 0, P.bcast[0]), (9, 1, P.bcast[1])])

    nl = [P.slot("nl0", "sync"), P.slot("nl1", "sync")]
    ns2 = [P.slot("ns2_0", "gpsimd"), P.slot("ns2_1", "gpsimd")]
    for c in range(DC):
        nl[c % 2].add(16)
        dneed = nl[c % 2].n

        def f_ld(sync, c=c, sl=nl[c % 2], slg=ns2[c % 2], gprev=ns2[c % 2].n):
            if gprev:
                sync.wait_ge(slg.h, gprev)
            sync.dma_start(out=P.ob[c % 2][:], in_=P.h_xn[:, c]).then_inc(sl.h, 16)
        P.emit("sync", f_ld)

        exn = [edge(P, "sv") for _ in range(3)]

        def f_xn(vector, c=c, dneed=dneed, sl=nl[c % 2], exn=exn):
            vector.wait_ge(sl.h, dneed)
            ob = P.ob[c % 2]
            exn[0](vector, vector.tensor_mul(out=ob[:], in0=ob[:],
                                             in1=P.bcast[0][:]))
            exn[1](vector, vector.tensor_sub(out=ob[:], in0=ob[:],
                                             in1=P.bcast[1][:]))
            exn[2](vector, vector.tensor_scalar(out=ob[:], in0=ob[:],
                                                scalar1=P.gb[:, c:c + 1],
                                                scalar2=P.gb[:, DC + c:DC + c + 1],
                                                op0=ALU.mult, op1=ALU.add))
            vector.scalar_tensor_tensor(out=P.xs[:, c], in0=ob[:],
                                        scalar=P.gb[:, 2 * DC + c:2 * DC + c + 1],
                                        in1=P.xs[:, c], op0=ALU.mult,
                                        op1=ALU.add).then_inc(P.sv.h, 1)
        sv.add(1)
        P.emit("vector", f_xn)
        vneed = sv.n

        ns2[c % 2].add(16)

        def f_sp(gpsimd, c=c, vneed=vneed, sl=ns2[c % 2]):
            gpsimd.wait_ge(P.sv.h, vneed)
            gpsimd.dma_start(out=P.h_xn[:, c], in_=P.ob[c % 2][:]).then_inc(sl.h, 16)
        P.emit("gpsimd", f_sp)
    P.barrier()


def ph_attn_proj(P):
    P.phase = "attn_proj"
    """delta = softplus(xn@Wdelta + b); B,C rows; DX = delta*xn -> HBM.
    xn streams from h_xn."""
    sd, st, sv, ss = P.sd, P.st, P.sv, P.ss
    P.push_scope()
    wstd = P.sbuf("wstd", [128, DC, D], BF16, scoped=True)
    wbc = P.sbuf("wbc", [128, DC, 2 * NST], BF16, scoped=True)

    for c in range(DC):
        def f_ld(sync, c=c):
            sync.dma_start(out=P.ob[0][:, 0:D], in_=P.wdelta[:, c]).then_inc(sd.h, 16)
            sync.dma_start(out=P.ob[1][:, 0:NST], in_=P.wB[:, c]).then_inc(sd.h, 16)
            sync.dma_start(out=P.ob[1][:, NST:2 * NST], in_=P.wC[:, c]).then_inc(sd.h, 16)
        sd.add(48)
        P.emit("sync", f_ld)
        need = sd.n

        def f_cast(vector, c=c, need=need):
            vector.wait_ge(sd.h, need)
            vector.tensor_copy(out=wstd[:, c], in_=P.ob[0][:, 0:D])
            vector.tensor_copy(out=wbc[:, c], in_=P.ob[1][:, 0:2 * NST])
            vector.engine_nop().then_inc(P.sv.h, 1)
        sv.add(1)
        P.emit("vector", f_cast)
        vneed = sv.n

        def f_w(sync, vneed=vneed):
            sync.wait_ge(P.sv.h, vneed)
        P.emit("sync", f_w)
    P.barrier()

    al = [P.slot("al0", "sync"), P.slot("al1", "sync")]
    aw = [P.slot("aw0", "gpsimd"), P.slot("aw1", "gpsimd")]
    ax = [P.slot("ax0", "sync"), P.slot("ax1", "sync")]
    for tt in range(TT4):
        cast_done = []
        for k in range(DC):
            qb = P.cb16[k % 2]
            al[k % 2].add(16)
            dneed = al[k % 2].n

            def f_xl(sync, k=k, tt=tt, sl=al[k % 2],
                     vprev=(cast_done[k - 2] if k >= 2 else 0)):
                if vprev:
                    sync.wait_ge(P.sv.h, vprev)
                sync.dma_start(out=P.ob[k % 2][:, 0:TQ],
                               in_=P.h_xn[:, k, tt * TQ:(tt + 1) * TQ]).then_inc(sl.h, 16)
            P.emit("sync", f_xl)

            def f_cast(vector, k=k, qb=qb, dneed=dneed, sl=al[k % 2],
                       tneed=(st.n if k >= 2 else 0)):
                vector.wait_ge(sl.h, dneed)
                if tneed:
                    vector.wait_ge(P.st.h, tneed - 1)
                vector.tensor_copy(out=qb[:, 0:TQ],
                                   in_=P.ob[k % 2][:, 0:TQ]).then_inc(P.sv.h, 1)
            sv.add(1)
            P.emit("vector", f_cast)
            cast_done.append(sv.n)
            vneed = sv.n

            def f_mm(tensor, k=k, qb=qb, vneed=vneed):
                tensor.wait_ge(P.sv.h, vneed)
                last = None
                for do in range(DC):
                    last = tensor.matmul(P.pmm[do][:],
                                         wstd[:, k, 128 * do:128 * (do + 1)],
                                         qb[:, 0:TQ], start=(k == 0),
                                         stop=(k == DC - 1))
                last.then_inc(P.st.h, 1)
            st.add(1)
            P.emit("tensor", f_mm)
        tneed = st.n

        wr_done = []
        for do in range(DC):
            def f_ev1(scalar, do=do, tneed=tneed, sl=aw[do % 2],
                      gneed=aw[do % 2].n):
                scalar.wait_ge(P.st.h, tneed)
                if gneed:
                    scalar.wait_ge(sl.h, gneed)
                scalar.copy(out=P.ob[do % 2][:, 0:TQ],
                            in_=P.pmm[do][:]).then_inc(P.ss.h, 1)
            ss.add(1)
            P.emit("scalar", f_ev1)
            sneed = ss.n
            ax[do % 2].add(16)
            dneed2 = ax[do % 2].n

            def f_xn2(sync, do=do, tt=tt, sl=ax[do % 2], slw=aw[do % 2],
                      gneed=aw[do % 2].n):
                if gneed:
                    sync.wait_ge(slw.h, gneed)
                sync.dma_start(out=P.cb[do % 2][:, 0:TQ],
                               in_=P.h_xn[:, do, tt * TQ:(tt + 1) * TQ]).then_inc(sl.h, 16)
            P.emit("sync", f_xn2)

            esp = edge(P, "sv")

            def f_sp(vector, do=do, sneed=sneed, slw=aw[(do - 1) % 2], esp=esp,
                     gneed=(aw[(do - 1) % 2].n if do >= 1 else 0)):
                vector.wait_ge(P.ss.h, sneed)
                if gneed:
                    vector.wait_ge(slw.h, gneed)
                z = P.ob[do % 2][:, 0:TQ]
                esp(vector, vector.tensor_scalar(out=z, in0=z,
                                                 scalar1=P.biases[:, do:do + 1],
                                                 scalar2=None, op0=ALU.add))
                vector.tensor_scalar(out=P.cb[2][:, 0:TQ], in0=z, scalar1=0.0,
                                     scalar2=None, op0=ALU.max).then_inc(P.sv.h, 1)
            sv.add(1)
            P.emit("vector", f_sp)
            vneed = sv.n

            es1, es2 = edge(P, "ss"), edge(P, "ss")

            def f_sp2(scalar, do=do, vneed=vneed, es1=es1, es2=es2):
                scalar.wait_ge(P.sv.h, vneed)
                z = P.ob[do % 2][:, 0:TQ]
                es1(scalar, scalar.activation(out=z, in_=z, func=AF.Abs))
                es2(scalar, scalar.activation(out=z, in_=z, func=AF.Exp,
                                              scale=-1.0))
                scalar.activation(out=z, in_=z, func=AF.Ln, bias=1.0).then_inc(P.ss.h, 1)
            ss.add(1)
            P.emit("scalar", f_sp2)
            sneed2 = ss.n

            es3 = edge(P, "sv")

            def f_sp3(vector, do=do, tt=tt, sneed2=sneed2, dneed2=dneed2,
                      sl=ax[do % 2], es3=es3):
                vector.wait_ge(P.ss.h, sneed2)
                vector.wait_ge(sl.h, dneed2)
                z = P.ob[do % 2][:, 0:TQ]
                es3(vector, vector.tensor_add(out=z, in0=z,
                                              in1=P.cb[2][:, 0:TQ]))
                vector.tensor_mul(out=P.cb[2][:, 0:TQ], in0=z,
                                  in1=P.cb[do % 2][:, 0:TQ]).then_inc(P.sv.h, 1)
            sv.add(1)
            P.emit("vector", f_sp3)
            vneed2 = sv.n

            aw[do % 2].add(32)

            def f_wr(gpsimd, do=do, tt=tt, vneed2=vneed2, sl=aw[do % 2]):
                gpsimd.wait_ge(P.sv.h, vneed2)
                gpsimd.dma_start(out=P.h_delta[:, do, tt * TQ:(tt + 1) * TQ],
                                 in_=P.ob[do % 2][:, 0:TQ]).then_inc(sl.h, 16)
                gpsimd.dma_start(out=P.h_dx[:, do, tt * TQ:(tt + 1) * TQ],
                                 in_=P.cb[2][:, 0:TQ]).then_inc(sl.h, 16)
            P.emit("gpsimd", f_wr)
            wr_done.append(0)
        P.barrier()

    for tt in range(TT4):
        cast_done = []
        for k in range(DC):
            al[k % 2].add(16)
            dneed = al[k % 2].n

            def f_xl(sync, k=k, tt=tt, sl=al[k % 2],
                     vprev=(cast_done[k - 2] if k >= 2 else 0)):
                if vprev:
                    sync.wait_ge(P.sv.h, vprev)
                sync.dma_start(out=P.ob[k % 2][:, 0:TQ],
                               in_=P.h_xn[:, k, tt * TQ:(tt + 1) * TQ]).then_inc(sl.h, 16)
            P.emit("sync", f_xl)

            def f_cst(vector, k=k, dneed=dneed, sl=al[k % 2],
                      tneed=(st.n if k >= 2 else 0)):
                vector.wait_ge(sl.h, dneed)
                if tneed:
                    vector.wait_ge(P.st.h, tneed - 1)
                vector.tensor_copy(out=P.cb16[k % 2][:, 0:TQ],
                                   in_=P.ob[k % 2][:, 0:TQ]).then_inc(P.sv.h, 1)
            sv.add(1)
            P.emit("vector", f_cst)
            cast_done.append(sv.n)
            vneed = sv.n

            def f_mmb(tensor, k=k, vneed=vneed):
                tensor.wait_ge(P.sv.h, vneed)
                tensor.matmul(P.pmm[0][0:2 * NST, :], wbc[:, k],
                              P.cb16[k % 2][:, 0:TQ],
                              start=(k == 0),
                              stop=(k == DC - 1)).then_inc(P.st.h, 1)
            st.add(1)
            P.emit("tensor", f_mmb)
        tneed = st.n

        def f_evb(vector, tt=tt, tneed=tneed):
            vector.wait_ge(P.st.h, tneed)
            vector.tensor_scalar(out=P.ob[0][0:2 * NST, 0:TQ],
                                 in0=P.pmm[0][0:2 * NST, :],
                                 scalar1=P.bcb[:, 0:1], scalar2=None,
                                 op0=ALU.add).then_inc(P.sv.h, 1)
        sv.add(1)
        P.emit("vector", f_evb)
        vneed = sv.n

        def f_wbc(sync, tt=tt, vneed=vneed):
            sync.wait_ge(P.sv.h, vneed)
            sync.dma_start(out=P.h_bc[:, tt * TQ:(tt + 1) * TQ],
                           in_=P.ob[0][0:2 * NST, 0:TQ]).then_inc(P.sd.h, 16)
        P.sd.add(16)
        P.emit("sync", f_wbc)
        P.barrier()
    P.pop_scope()


def ph_attn_scan(P):
    P.phase = "attn_scan"
    """the mamba middle: for n-groups of 4, stream delta/DX per d-chunk,
    scan over full T, accumulate y into xs."""
    sd, sv, ss, sg = P.sd, P.sv, P.ss, P.sg
    P.push_scope()
    nbc = [P.sbuf(f"nbc{i}", [128, T], scoped=True) for i in range(4)]
    wsc = [P.sbuf(f"wsc{i}", [128, T], scoped=True) for i in range(2)]
    usc = [P.sbuf(f"usc{i}", [128, T], scoped=True) for i in range(2)]
    db = [P.bcast[0], P.bcast[1]]
    xb = [P.cb[0], P.cb[1]]
    ab = [P.ob[0], P.ob[1]]
    hb = P.cb[2]

    jj = 0
    scandone = []
    gpdone = []
    for ng in range(8):
        if ng:
            P.barrier()
        for j in range(2):
            n = 2 * ng + j
            bcast_row(P, P.h_bc[n:n + 1, :], nbc[2 * j])
            bcast_row(P, P.h_bc[NST + n:NST + n + 1, :], nbc[2 * j + 1])
        P.barrier()
        cdone = []
        sls = [P.slot("as0", "sync"), P.slot("as1", "sync")]
        for c in range(DC):
            par = c % 2
            ssprev, svprev = cdone[c - 2] if c >= 2 else (0, 0)
            sls[par].add(32)

            def f_ld(sync, c=c, par=par, ssprev=ssprev, svprev=svprev,
                     sl=sls[par]):
                if ssprev:
                    sync.wait_ge(P.ss.h, ssprev)
                if svprev:
                    sync.wait_ge(P.sv.h, svprev)
                sync.dma_start(out=db[par][:],
                               in_=P.h_delta[:, c]).then_inc(sl.h, 16)
                sync.dma_start(out=xb[par][:],
                               in_=P.h_dx[:, c]).then_inc(sl.h, 16)
            P.emit("sync", f_ld)
            dneed = sls[par].n
            for j in range(2):
                n = 2 * ng + j
                col = c * NST + n
                jp = jj % 2
                aprev = scandone[jj - 2] if jj >= 2 else 0

                def f_a(scalar, par=par, col=col, dneed=dneed, jp=jp,
                        aprev=aprev, first=(j == 0), sl=sls[par]):
                    if first:
                        scalar.wait_ge(sl.h, dneed)
                    if aprev:
                        scalar.wait_ge(P.sv.h, aprev)
                    scalar.activation(out=ab[jp][:], in_=db[par][:],
                                      func=AF.Exp,
                                      scale=P.acols[:, col:col + 1]
                                      ).then_inc(P.ss.h, 1)
                ss.add(1)
                P.emit("scalar", f_a)
                sneed = ss.n

                b0 = sv.n
                eu1, eu2, eu3 = edge(P, "sv"), edge(P, "sv"), edge(P, "sv")
                wprev = gpdone[jj - 2] if jj >= 2 else 0

                def f_u(vector, j=j, par=par, jp=jp, dneed=dneed, sneed=sneed,
                        wprev=wprev, eu1=eu1, eu2=eu2, eu3=eu3, sl=sls[par]):
                    vector.wait_ge(sl.h, dneed)
                    eu1(vector, vector.tensor_mul(out=usc[jp][:],
                                                  in0=xb[par][:],
                                                  in1=nbc[2 * j][:]))
                    vector.wait_ge(P.ss.h, sneed)
                    eu2(vector, vector.tensor_tensor_scan(
                        out=hb[:], data0=ab[jp][:], data1=usc[jp][:],
                        initial=0.0, op0=ALU.mult, op1=ALU.add))
                    if wprev:
                        vector.wait_ge(P.sg.h, wprev)
                    eu3(vector, vector.tensor_mul(out=wsc[jp][:], in0=hb[:],
                                                  in1=nbc[2 * j + 1][:]))
                P.emit("vector", f_u)
                scandone.append(b0 + 2)
                vneed = sv.n

                prev_gp = gpdone[jj - 1] if (j == 1 and jj >= 1) else 0

                def f_y(gpsimd, c=c, jp=jp, vneed=vneed, prev_gp=prev_gp):
                    gpsimd.wait_ge(P.sv.h, vneed)
                    if prev_gp:
                        gpsimd.wait_ge(P.sg.h, prev_gp)
                    gpsimd.tensor_add(out=P.xs[:, c], in0=P.xs[:, c],
                                      in1=wsc[jp][:]).then_inc(P.sg.h, 1)
                sg.add(1)
                P.emit("gpsimd", f_y)
                gpdone.append(sg.n)
                jj += 1
            cdone.append((ss.n, sv.n))
    P.barrier()
    P.pop_scope()


def ph_debug_out(P, src):
    sd = P.sd
    sd.add(16)
    need = sd.n

    def f(sync, need=need):
        sync.dma_start(out=P.y_out, in_=src[:]).then_inc(sd.h, 16)
        sync.wait_ge(sd.h, need)
    P.emit("sync", f)
    P.barrier()


def ph_final_wait(P):
    P.phase = "final"
    need = P.sd.n

    def f(sync, need=need):
        sync.wait_ge(P.sd.h, need)
    P.emit("sync", f)
    gneed = P.gd.n

    def f2(gpsimd, gneed=gneed):
        gpsimd.wait_ge(P.gd.h, gneed)
    P.emit("gpsimd", f2)
    P.barrier()



def build_program(debug_stage=None):
    P = Prog(debug_stage)
    ph_load(P)
    ph_dequant(P, "g")
    ph_dequant(P, "u")
    ph_dequant(P, "d")
    if debug_stage == "deqr":
        sd = P.sd
        sd.add(16)
        need = sd.n

        def f_t(sync, need=need):
            sync.dma_start(out=P.y_out[0:32, 0, 0:T], in_=P.h_rg[0:32, 0:T]).then_inc(sd.h, 16)
            sync.wait_ge(sd.h, need)
        P.emit("sync", f_t)
        P.barrier()
        ph_final_wait(P)
        return P
    if debug_stage == "deqs":
        sd = P.sd
        sd.add(16)
        need = sd.n

        def f_t(sync, need=need):
            sync.dma_start(out=P.y_out[0:32, 0, 0:T], in_=P.h_sg[0:32, 0:T]).then_inc(sd.h, 16)
            sync.wait_ge(sd.h, need)
        P.emit("sync", f_t)
        P.barrier()
        ph_final_wait(P)
        return P
    if debug_stage == "deq":
        sd, sv = P.sd, P.sv
        for c in range(DC):
            def f_ld(sync, c=c):
                sync.dma_start(out=P.cb16[0][:, 0:T],
                               in_=P.h_wgD[:, c, 0:T]).then_inc(sd.h, 16)
            sd.add(16)
            P.emit("sync", f_ld)
            need = sd.n

            def f_cast(vector, c=c, need=need):
                vector.wait_ge(sd.h, need)
                vector.tensor_copy(out=P.xs[:, c], in_=P.cb16[0][:, 0:T]).then_inc(P.sv.h, 1)
            sv.add(1)
            P.emit("vector", f_cast)
            vneed = sv.n

            def f_w(sync, vneed=vneed):
                sync.wait_ge(P.sv.h, vneed)
            P.emit("sync", f_w)
        P.barrier()
        ph_debug_out(P, P.xs)
        ph_final_wait(P)
        return P
    if debug_stage and debug_stage.startswith("f1_"):
        st = debug_stage[3:]
        off = 0
        if st.endswith("B"):
            off, st = 8, st[:-1]
        elif st.endswith("C"):
            off, st = 16, st[:-1]
        ph_ffn(P, 0, stop=st)
        dbg = P.nc.dram_tensor("dbg_sc", [1, 32], F32, kind="ExternalOutput").ap()
        P.sd.add(16)
        dneed = P.sd.n

        def f_sc(sync, dneed=dneed):
            sync.dma_start(out=dbg[0:1, 0:24], in_=P.sc1[0:1, 0:24]).then_inc(P.sd.h, 16)
            sync.wait_ge(P.sd.h, dneed)
        P.emit("sync", f_sc)
        src = {"gate": P.h_gate, "pchain": P.h_p, "up": P.h_up,
               "opass": P.h_o}.get(st)
        if src is not None:
            P.sd.add(16)
            need = P.sd.n

            def f_h(sync, src=src, off=off, need=need):
                sync.dma_start(out=P.y_out[:, 0:DC, 0:T],
                               in_=src[:, off:off + DC, 0:T]).then_inc(P.sd.h, 16)
                sync.wait_ge(P.sd.h, need)
            P.emit("sync", f_h)
            P.barrier()
        else:
            ph_debug_out(P, P.xs)
        ph_final_wait(P)
        return P
    ph_ffn(P, 0)
    if debug_stage == "x1":
        ph_debug_out(P, P.xs)
        ph_final_wait(P)
        return P
    ph_attn_norms(P)
    ph_attn_proj(P)
    if debug_stage == "attn_pre":
        ph_debug_out(P, P.xs)
        ph_final_wait(P)
        return P
    ph_attn_scan(P)
    if debug_stage == "x2":
        ph_debug_out(P, P.xs)
        ph_final_wait(P)
        return P
    ph_ffn(P, 1)
    ph_final_wait(P)
    return P


def _host_layout(inputs):
    x = np.asarray(inputs["L0_attn_x"])
    wg = np.asarray(inputs["L0_ffn_w_gate"])
    wu = np.asarray(inputs["L0_ffn_w_up"])
    wd = np.asarray(inputs["L0_ffn_w_down"])
    wdel = np.asarray(inputs["L0_attn_W_delta"])
    wB = np.asarray(inputs["L0_attn_W_B"])
    wC = np.asarray(inputs["L0_attn_W_C"])
    alog = np.asarray(inputs["L0_attn_A_log"])
    bdel = np.asarray(inputs["L0_attn_b_delta"])
    bB = np.asarray(inputs["L0_attn_b_B"])
    bC = np.asarray(inputs["L0_attn_b_C"])
    gamma = np.asarray(inputs["L0_attn_gamma"])
    beta = np.asarray(inputs["L0_attn_beta"])
    dskip = np.asarray(inputs["L0_attn_D_skip"])

    def chunks_T(w):
        K, M = w.shape
        return np.ascontiguousarray(w.reshape(K // 128, 128, M).transpose(1, 0, 2))

    shared = {
        "wgT": chunks_T(np.ascontiguousarray(wg.T)),
        "wuT": chunks_T(np.ascontiguousarray(wu.T)),
        "wdT": chunks_T(np.ascontiguousarray(wd.T)),
        "wgN": chunks_T(wg),
        "wuN": chunks_T(wu),
        "wdN": chunks_T(wd),
        "wdelta": chunks_T(wdel),
        "wB": chunks_T(wB),
        "wC": chunks_T(wC),
        "alog": np.ascontiguousarray(
            alog.reshape(DC, 128, NST).transpose(1, 0, 2).reshape(128, DC * NST)),
        "bdelta": np.ascontiguousarray(bdel.reshape(DC, 128).T),
        "bBC": np.concatenate([bB, bC]).reshape(2 * NST, 1),
        "gamma": np.ascontiguousarray(gamma.reshape(DC, 128).T),
        "beta": np.ascontiguousarray(beta.reshape(DC, 128).T),
        "dskip": np.ascontiguousarray(dskip.reshape(DC, 128).T),
        "ident": np.eye(128, dtype=np.float32),
        "fconst": np.tile(np.array([[65536.0, MAGIC, EPS, -0.5]], dtype=np.float32),
                          (128, 1)),
    }
    shared = {k: np.ascontiguousarray(v, dtype=np.float32) for k, v in shared.items()}

    in_maps = []
    for c in range(NCORES):
        xc = np.ascontiguousarray(np.asarray(x[c]).T)
        xc = np.ascontiguousarray(xc.reshape(DC, 128, T).transpose(1, 0, 2))
        m = dict(shared)
        m["x_fm"] = xc
        in_maps.append(m)
    return in_maps


def kernel(**inputs):
    in_maps = _host_layout(inputs)
    P = build_program(DEBUG_STAGE)
    nc = P.finalize()
    res = run_bass_kernel_spmd(nc, in_maps, core_ids=list(range(NCORES)))
    if DEBUG_STAGE:
        _DBG.clear()
        for c in range(NCORES):
            _DBG[c] = {k: np.asarray(v) for k, v in res.results[c].items()}
    outs = []
    for c in range(NCORES):
        y = np.asarray(res.results[c]["y_fm"])
        y = y.transpose(1, 0, 2).reshape(D, T).T
        outs.append(y)
    return np.ascontiguousarray(np.stack(outs, axis=0), dtype=np.float32)
